# version 05
# baseline (speedup 1.0000x reference)
import contextlib
import math
import numpy as np
import ml_dtypes
import concourse.bass as bass
import concourse.mybir as mybir
from concourse.bass_utils import run_bass_kernel_spmd

F32 = mybir.dt.float32
BF16 = mybir.dt.bfloat16
AF = mybir.ActivationFunctionType
ALU = mybir.AluOpType
AX = mybir.AxisListType
BFNP = ml_dtypes.bfloat16

H, DN, DV, D, DFF, NB, S = 32, 128, 128, 4096, 11008, 4, 2048
SCALE = 1.0 / math.sqrt(192.0)
EPS = 1e-5
NT = 16
OWN_KT = [15, 12, 11, 8, 7, 4, 3, 0]
EXT = [16 - 2 * j for j in range(8)]
NKT = [128 * math.ceil((16 - kt) / 2) for kt in range(16)]
NFT = DFF // 128
FT_STRIPS = [(0, 32), (32, 64), (64, 86)]
WO_RING = 9


def core_layout(par):
    if par == 0:
        tiles_true = [15, 12, 11, 8, 7, 4, 3, 0]
        perm = list(range(16))
    else:
        tiles_true = [14, 13, 10, 9, 6, 5, 2, 1]
        perm = []
        for g in range(4):
            perm += [4 * g + 1, 4 * g + 0, 4 * g + 3, 4 * g + 2]
    return tiles_true, perm


def build_masks(par):
    tiles_true, perm = core_layout(par)
    masks = np.zeros((16, 128, 128), dtype=np.float32)
    for kt in range(16):
        j = (15 - kt) // 2
        qpos = tiles_true[j] * 128 + np.arange(128)
        kpos = perm[kt] * 128 + np.arange(128)
        masks[kt] = (kpos[:, None] <= qpos[None, :]).astype(np.float32)
    for j in range(8):
        qpos = tiles_true[j] * 128 + np.arange(128)
        cov = np.zeros((2048, 128))
        for kt in range(EXT[j]):
            kpos = perm[kt] * 128 + np.arange(128)
            w = masks[kt] if kt in (EXT[j] - 2, EXT[j] - 1) else np.ones((128, 128))
            cov[kpos] += w
        want = (np.arange(2048)[:, None] <= qpos[None, :]).astype(np.float32)
        assert np.array_equal(cov, want), f"mask coverage par={par} slot={j}"
    return masks.astype(BFNP)


class Ctr:
    def __init__(self, handle):
        self.h = handle
        self.v = 0

    @property
    def tok(self):
        return (self.h, self.v)


class Prog:
    ENGINES = ("tensor", "vector", "scalar", "gpsimd", "sync")

    def __init__(self):
        self.q = {e: [] for e in self.ENGINES}
        self.waited = {e: {} for e in self.ENGINES}

    def emit(self, eng, waits, build, inc=None, inc_n=1):
        ws = []
        seen = self.waited[eng]
        for t in waits:
            if t is None:
                continue
            h, v = t
            if v <= 0:
                continue
            key = id(h)
            if seen.get(key, -1) >= v:
                continue
            seen[key] = v
            ws.append((h, v))
        tok = None
        if inc is not None:
            inc.v += inc_n
            tok = (inc.h, inc.v)
        ih = inc.h if inc is not None else None

        def fn(e, ws=tuple(ws), build=build, ih=ih, iv=inc_n):
            for (h, v) in ws:
                e.wait_ge(h, v)
            ins = build(e)
            if ih is not None:
                ins.then_inc(ih, iv)
        self.q[eng].append(fn)
        return tok

    def margin(self, tok, ctr):
        if tok is None or ctr.v < tok[1] + 16:
            return tok
        return (tok[0], tok[1] + 16)

    def raw_wait(self, eng, tok):
        h, v = tok
        if self.waited[eng].get(id(h), -1) >= v:
            return
        self.waited[eng][id(h)] = v
        self.q[eng].append(lambda e, h=h, v=v: e.wait_ge(h, v))


def build_nc():
    nc = bass.Bass("TRN2", target_bir_lowering=False)

    x_own = nc.dram_tensor("x_own", [1024, D], F32, kind="ExternalInput").ap()
    x_full = nc.dram_tensor("x_full", [S, D], F32, kind="ExternalInput").ap()
    wq = nc.dram_tensor("wq", [D, 4096], F32, kind="ExternalInput").ap()
    wk = nc.dram_tensor("wk", [D, 4096], F32, kind="ExternalInput").ap()
    wv = nc.dram_tensor("wv", [D, 4096], F32, kind="ExternalInput").ap()
    wo = nc.dram_tensor("wo", [4096, D], F32, kind="ExternalInput").ap()
    wgT = nc.dram_tensor("wgT", [D, DFF], F32, kind="ExternalInput").ap()
    wuT = nc.dram_tensor("wuT", [D, DFF], F32, kind="ExternalInput").ap()
    wdT = nc.dram_tensor("wdT", [DFF, D], F32, kind="ExternalInput").ap()
    masks_d = nc.dram_tensor("masks", [16, 128, 128], BF16, kind="ExternalInput").ap()
    out_d = nc.dram_tensor("out", [1024, D], F32, kind="ExternalOutput").ap()

    scr_k = nc.dram_tensor("scr_k", [H, 128, S], BF16, kind="Internal").ap()
    scr_v = nc.dram_tensor("scr_v", [H, 16, 128, 128], BF16, kind="Internal").ap()
    scr_y = nc.dram_tensor("scr_y", [1024, D], F32, kind="Internal").ap()
    scr_xatt = nc.dram_tensor("scr_xatt", [1024, D], F32, kind="Internal").ap()
    scr_h = nc.dram_tensor("scr_h", [NFT, 128, 1024], BF16, kind="Internal").ap()
    scr_y2 = nc.dram_tensor("scr_y2", [1024, D], F32, kind="Internal").ap()
    scr_junk = nc.dram_tensor("scr_junk", [128, 64], F32, kind="Internal").ap()

    ident_d = nc.inline_tensor(np.eye(128, dtype=BFNP).view(np.uint16), name="ident")

    P = Prog()
    ENG5 = Prog.ENGINES

    with contextlib.ExitStack() as es:
        def sbuf(stack, name, shape, dtype, side="right"):
            return stack.enter_context(nc.sbuf_tensor(name, shape, dtype, side=side))

        ident = sbuf(es, "ident_sb", [128, 128], BF16, "left")
        ones_col = sbuf(es, "ones_col", [128, 1], BF16, "left")
        ones_row = sbuf(es, "ones_row", [1, 128], BF16, "left")
        ones_row_f = sbuf(es, "ones_row_f", [1, 128], F32, "left")
        ones_sq = sbuf(es, "ones_sq", [128, 128], BF16, "left")
        stats = sbuf(es, "stats", [128, 32], F32, "left")
        negm = sbuf(es, "negm", [128, H], F32, "left")
        eps_t = sbuf(es, "eps_t", [128, 1], F32, "left")
        tiny_t = sbuf(es, "tiny_t", [128, 1], F32, "left")
        nrm_acc = sbuf(es, "nrm_acc", [128, 2 * H], F32, "left")
        zeros64 = sbuf(es, "zeros64", [128, 64], BF16, "left")

        sG = Ctr(es.enter_context(nc.semaphore("sG")))
        sX = Ctr(es.enter_context(nc.semaphore("sX")))
        sC = Ctr(es.enter_context(nc.semaphore("sC")))
        sT = Ctr(es.enter_context(nc.semaphore("sT")))
        sV = Ctr(es.enter_context(nc.semaphore("sV")))
        sA = Ctr(es.enter_context(nc.semaphore("sA")))
        sB = Ctr(es.enter_context(nc.semaphore("sB")))

        def barrier():
            sB.v += 5
            tgt = sB.v
            for e in ENG5:
                P.emit(e, [], lambda en: en.sem_inc(sB.h, 1))
            for e in ENG5:
                P.raw_wait(e, (sB.h, tgt))

        def ln_stats(xin_ap, ready_toks, c0, sqscr):
            ssum = stats[:, c0:c0 + 1]
            mn = stats[:, c0 + 1:c0 + 2]
            msq = stats[:, c0 + 2:c0 + 3]
            varp = stats[:, c0 + 3:c0 + 4]
            rstd = stats[:, c0 + 4:c0 + 5]
            ssq = stats[:, c0 + 5:c0 + 6]
            fd = xin_ap.shape[1]
            t0 = P.emit("vector", ready_toks, lambda e: e.reduce_sum(
                ssum, xin_ap, axis=AX.X), inc=sV)
            t1 = P.emit("vector", [t0], lambda e: e.tensor_scalar_mul(
                mn, ssum, -1.0 / fd), inc=sV)
            tv = P.emit("vector", [t1], lambda e: e.tensor_mul(msq, mn, mn), inc=sV)
            ta = P.emit("scalar", ready_toks, lambda e: e.activation(
                sqscr[:, 0:fd], xin_ap, AF.Square, accum_out=ssq), inc=sA)
            tv2 = P.emit("vector", [tv, ta], lambda e: e.scalar_tensor_tensor(
                varp, ssq, 1.0 / fd, msq, ALU.mult, ALU.subtract), inc=sV)
            tl = P.emit("scalar", [tv2], lambda e: e.activation(
                rstd, varp, AF.Ln, bias=eps_t[:, 0:1]), inc=sA)
            tr = P.emit("scalar", [tl], lambda e: e.activation(
                rstd, rstd, AF.Exp, scale=-0.5), inc=sA)
            return tr, mn, rstd

        t_id = P.emit("sync", [], lambda e: e.dma_start(
            out=ident[:], in_=ident_d.ap().bitcast(BF16)), inc=sX, inc_n=16)
        P.emit("vector", [], lambda e: e.memset(ones_col[:], 1.0))
        P.emit("vector", [], lambda e: e.memset(ones_row[:], 1.0))
        P.emit("vector", [], lambda e: e.memset(ones_row_f[:], 1.0))
        P.emit("vector", [], lambda e: e.memset(eps_t[:], EPS))
        P.emit("vector", [], lambda e: e.memset(tiny_t[:], 1e-30))
        P.emit("vector", [], lambda e: e.memset(zeros64[:], 0.0))
        t_ones = P.emit("vector", [], lambda e: e.memset(ones_sq[:], 1.0), inc=sV)
        barrier()

        es_xlnT = contextlib.ExitStack()
        xlnT = sbuf(es_xlnT, "xlnT", [128, 32 * S], BF16, "left")

        with contextlib.ExitStack() as st1a:
            xin = [sbuf(st1a, f"xin{i}", [128, D], F32) for i in range(2)]
            xln_bf = [sbuf(st1a, f"xlnbf{i}", [128, D], BF16) for i in range(2)]
            sqscr = sbuf(st1a, "sqscr1a", [128, D], BF16)
            trps = [st1a.enter_context(nc.psum_tensor(f"trps{i}", [128, 1024], BF16))
                    for i in range(2)]
            trjunk = st1a.enter_context(nc.psum_tensor("trjunk", [128, 128], BF16))
            xin_free = [None, None]
            cp_done = {}

            def emit_cp(gidx, gp, kt, g, wit):
                src = trps[gp][:].rearrange("p (u n) -> p u n", n=128)
                dst = xlnT[:].rearrange("p (a t) -> p a t", t=S)[
                    :, g * 8:(g + 1) * 8, kt * 128:(kt + 1) * 128]
                cp_done[gidx] = P.emit(
                    "vector", [wit], lambda e, src=src, dst=dst:
                    e.tensor_copy(dst, src), inc=sV)

            for kt in range(NT):
                pp = kt % 2
                xin_tok = P.emit("sync", [xin_free[pp]],
                                 lambda e, kt=kt, pp=pp: e.dma_start(
                                     out=xin[pp][:],
                                     in_=x_full[kt * 128:(kt + 1) * 128, :]),
                                 inc=sX, inc_n=16)
                tr, mn, rstd = ln_stats(xin[pp][:], [xin_tok], pp * 6, sqscr)
                xln_tok = P.emit("vector", [tr],
                                 lambda e, pp=pp, mn=mn, rstd=rstd: e.tensor_scalar(
                                     xln_bf[pp][:], xin[pp][:], mn, rstd,
                                     ALU.add, ALU.mult), inc=sV)
                xin_free[pp] = xln_tok
                gtok = {}
                for g in range(4):
                    gp = g % 2
                    w = [xln_tok, t_id, cp_done.get(kt * 4 + g - 2)]
                    tok = None
                    for u in range(8):
                        a = g * 8 + u
                        tok = P.emit("tensor", w if u == 0 else [],
                                     lambda e, pp=pp, a=a, gp=gp, u=u: e.transpose(
                                         trps[gp][:, u * 128:(u + 1) * 128],
                                         xln_bf[pp][:, a * 128:(a + 1) * 128],
                                         ident[:]),
                                     inc=(sT if u == 7 else None))
                    gtok[g] = tok
                    if g > 0:
                        emit_cp(kt * 4 + g - 1, (g - 1) % 2, kt, g - 1, tok)
                dummy = P.emit("tensor", [], lambda e, pp=pp: e.transpose(
                    trjunk[:], xln_bf[pp][:, 0:128], ident[:]), inc=sT)
                emit_cp(kt * 4 + 3, 1, kt, 3, dummy)
            xlnT_ready = sV.tok
        barrier()

        with contextlib.ExitStack() as st1b:
            wvb = [sbuf(st1b, f"wvb{i}", [128, 32, 512], BF16) for i in range(2)]
            vstage = [sbuf(st1b, f"vstage{i}", [128, 512], BF16) for i in range(2)]
            ps_v = [st1b.enter_context(nc.psum_tensor(f"psv{i}", [128, 512], F32))
                    for i in range(2)]
            wv_tok = [None, None]
            wv_free = [None, None]
            mm_done = {}
            ev_done = {}
            st_done = {}

            def emit_v_evac(it, wit):
                vp = it % 2
                n, kt = it // 16, it % 16
                ev_done[it] = P.emit("scalar", [wit, st_done.get(it - 2)],
                                     lambda e, vp=vp: e.copy(
                                         vstage[vp][:], ps_v[vp][:]), inc=sA)
                last = None
                for qh in range(4):
                    last = P.emit("sync", [ev_done[it]] if qh == 0 else [],
                                  lambda e, n=n, kt=kt, qh=qh, vp=vp: e.dma_start(
                                      out=scr_v[n * 4 + qh, kt],
                                      in_=vstage[vp][:, qh * 128:(qh + 1) * 128]),
                                  inc=sX, inc_n=16)
                st_done[it] = last

            for n in range(8):
                pp = n % 2
                wv_tok[pp] = P.emit("gpsimd", [wv_free[pp]],
                                    lambda e, n=n, pp=pp: e.dma_start(
                                        out=wvb[pp][:],
                                        in_=wv[:, n * 512:(n + 1) * 512].rearrange(
                                            "(a p) n -> p a n", p=128)),
                                    inc=sG, inc_n=16)
                for kt in range(NT):
                    it = n * 16 + kt
                    vp = it % 2
                    w = [P.margin(wv_tok[pp], sG), xlnT_ready, ev_done.get(it - 2)]
                    tok = None
                    for dk in range(32):
                        tok = P.emit("tensor", w if dk == 0 else [],
                                     lambda e, dk=dk, kt=kt, pp=pp, vp=vp: e.matmul(
                                         ps_v[vp][:],
                                         xlnT[:, dk * S + kt * 128:
                                              dk * S + (kt + 1) * 128],
                                         wvb[pp][:, dk, :],
                                         start=(dk == 0), stop=(dk == 31)),
                                     inc=(sT if dk == 31 else None))
                    mm_done[it] = tok
                    if it > 0:
                        emit_v_evac(it - 1, tok)
                wv_free[pp] = mm_done[n * 16 + 15]
            wit = P.emit("tensor", [], lambda e: e.matmul(
                ps_v[1][:, 0:64], ident[:], zeros64[:], start=False, stop=True, skip_group_check=True),
                inc=sT)
            emit_v_evac(127, wit)
        barrier()

        with contextlib.ExitStack() as st1c:
            wkb = [sbuf(st1c, f"wkb{i}", [128, 32, 128], BF16) for i in range(2)]
            kstage = sbuf(st1c, "kstage", [128, S], BF16)
            ksq = sbuf(st1c, "ksq", [128, S], BF16)
            ps_k = st1c.enter_context(nc.psum_tensor("psk", [128, S], F32))
            ps_n = st1c.enter_context(nc.psum_tensor("psn", [128, 2048], F32))
            wk_tok = [None, None]
            wk_free = [None, None]
            ev_tok = None
            st_tok = None
            prev_nm = None
            for h in range(H):
                pp = h % 2
                wk_tok[pp] = P.emit("gpsimd", [wk_free[pp]],
                                    lambda e, h=h, pp=pp: e.dma_start(
                                        out=wkb[pp][:],
                                        in_=wk[:, h * 128:(h + 1) * 128].rearrange(
                                            "(a p) n -> p a n", p=128)),
                                    inc=sG, inc_n=16)
                w = [P.margin(wk_tok[pp], sG), ev_tok]
                tok = None
                for dk in range(32):
                    for c in range(4):
                        tok = P.emit("tensor", w if (dk == 0 and c == 0) else [],
                                     lambda e, dk=dk, c=c, pp=pp: e.matmul(
                                         ps_k[:, c * 512:(c + 1) * 512],
                                         wkb[pp][:, dk, :],
                                         xlnT[:, dk * S + c * 512:
                                              dk * S + (c + 1) * 512],
                                         start=(dk == 0), stop=(dk == 31)),
                                     inc=(sT if (dk == 31 and c == 3) else None))
                wk_free[pp] = tok
                wit = P.emit("tensor", [], lambda e: e.matmul(
                    ps_k[:, 0:64], ident[:], zeros64[:], start=False, stop=True, skip_group_check=True),
                    inc=sT)
                ev_tok = P.emit("scalar", [wit, st_tok], lambda e: e.copy(
                    kstage[:], ps_k[:]), inc=sA)
                st_tok = P.emit("sync", [ev_tok], lambda e, h=h: e.dma_start(
                    out=scr_k[h], in_=kstage[:]), inc=sX, inc_n=16)
                sq_tok = P.emit("vector", [ev_tok, prev_nm],
                                lambda e: e.tensor_mul(ksq[:], kstage[:], kstage[:]),
                                inc=sV)
                km = nrm_acc[:, H + h:H + h + 1]
                red = stats[:, 24:25]
                w_nm = [sq_tok, prev_nm]
                for c in range(4):
                    P.emit("tensor", w_nm if c == 0 else [],
                           lambda e, c=c: e.matmul(
                               ps_n[:, c * 512:(c + 1) * 512], ones_sq[:],
                               ksq[:, c * 512:(c + 1) * 512],
                               start=True, stop=True))
                witn = P.emit("tensor", [], lambda e: e.matmul(
                    ps_n[:, 0:64], ident[:], zeros64[:],
                    start=False, stop=True, skip_group_check=True), inc=sT)
                rm = P.emit("vector", [witn], lambda e, red=red:
                            e.reduce_max(red, ps_n[:], axis=AX.X), inc=sV)
                prev_nm = P.emit("vector", [rm], lambda e, km=km, red=red:
                                 e.tensor_copy(km, red), inc=sV)
        barrier()

        es_qT = contextlib.ExitStack()
        qT = sbuf(es_qT, "qT", [128, H * 1024], BF16)
        with contextlib.ExitStack() as st1d:
            wqb = [sbuf(st1d, f"wqb{i}", [128, 16, 128], BF16) for i in range(2)]
            qsq = sbuf(st1d, "qsq", [128, 1024], BF16)
            ps_q = [st1d.enter_context(nc.psum_tensor(f"psq{i}", [128, 1024], F32))
                    for i in range(2)]
            ps_n2 = st1d.enter_context(nc.psum_tensor("psn2", [128, 1024], F32))
            pcol = []
            for j in range(8):
                kt = OWN_KT[j]
                pcol.append((kt // 4) * 128 if kt % 4 == 0 else 512 + (kt // 4) * 128)
            wq_tok = {}
            wq_free = {}
            cp_tok = {}
            prev_nm = None
            for h in range(H):
                qp = h % 2
                for dh in range(2):
                    ch = h * 2 + dh
                    bp = ch % 2
                    wq_tok[ch] = P.emit("gpsimd", [wq_free.get(ch - 2)],
                                        lambda e, h=h, dh=dh, bp=bp: e.dma_start(
                                            out=wqb[bp][:],
                                            in_=wq[dh * 2048:(dh + 1) * 2048,
                                                   h * 128:(h + 1) * 128].rearrange(
                                                "(a p) n -> p a n", p=128)),
                                        inc=sG, inc_n=16)
                tok = None
                for dh in range(2):
                    ch = h * 2 + dh
                    bp = ch % 2
                    w = [P.margin(wq_tok[ch], sG), cp_tok.get(h - 2)]
                    for dk in range(16):
                        dkg = dh * 16 + dk
                        rhs_a = xlnT[:, dkg * S:(dkg + 1) * S].rearrange(
                            "p (k n) -> p k n", n=128)
                        for half in range(2):
                            sel = rhs_a[:, (0 if half == 0 else 3)::4, :]
                            tok = P.emit(
                                "tensor", w if (dk == 0 and half == 0) else [],
                                lambda e, dkg=dkg, half=half, qp=qp, bp=bp, sel=sel:
                                e.matmul(ps_q[qp][:, half * 512:(half + 1) * 512],
                                         wqb[bp][:, dkg % 16, :], sel,
                                         start=(dkg == 0), stop=(dkg == 31)),
                                inc=(sT if (dkg == 31 and half == 1) else None))
                    wq_free[ch] = tok if dh == 1 else None
                mm_tok = tok
                wq_free[h * 2] = mm_tok
                wq_free[h * 2 + 1] = mm_tok
                wit = P.emit("tensor", [], lambda e, qp=qp: e.matmul(
                    ps_q[qp][:, 0:64], ident[:], zeros64[:],
                    start=False, stop=True, skip_group_check=True), inc=sT)
                last = None
                for j in range(8):
                    last = P.emit("vector", [wit] if j == 0 else [],
                                  lambda e, j=j, h=h, qp=qp: e.tensor_copy(
                                      qT[:, h * 1024 + j * 128:
                                         h * 1024 + (j + 1) * 128],
                                      ps_q[qp][:, pcol[j]:pcol[j] + 128]),
                                  inc=(sV if j == 7 else None))
                cp_tok[h] = last
                sq_tok = P.emit("vector", [last, prev_nm],
                                lambda e, h=h: e.tensor_mul(
                                    qsq[:], qT[:, h * 1024:(h + 1) * 1024],
                                    qT[:, h * 1024:(h + 1) * 1024]), inc=sV)
                qm = nrm_acc[:, h:h + 1]
                red = stats[:, 25:26]
                w_nm = [sq_tok, prev_nm]
                for c in range(2):
                    P.emit("tensor", w_nm if c == 0 else [],
                           lambda e, c=c: e.matmul(
                               ps_n2[:, c * 512:(c + 1) * 512], ones_sq[:],
                               qsq[:, c * 512:(c + 1) * 512],
                               start=True, stop=True))
                witn = P.emit("tensor", [], lambda e: e.matmul(
                    ps_n2[:, 0:64], ident[:], zeros64[:],
                    start=False, stop=True, skip_group_check=True), inc=sT)
                rm = P.emit("vector", [witn], lambda e, red=red:
                            e.reduce_max(red, ps_n2[:], axis=AX.X), inc=sV)
                gate = P.emit("vector", [rm], lambda e, qm=qm, red=red:
                              e.tensor_copy(qm, red), inc=sV)
                prev_nm = gate
                t1 = stats[:, 26:27]
                tmul = P.emit("vector", [gate], lambda e, qm=qm, h=h, t1=t1:
                              e.tensor_mul(t1, qm, nrm_acc[:, H + h:H + h + 1]),
                              inc=sV)
                tln = P.emit("scalar", [tmul], lambda e, t1=t1: e.activation(
                    t1, t1, AF.Ln, bias=tiny_t[:, 0:1]), inc=sA)
                tq = P.emit("scalar", [tln], lambda e, t1=t1: e.activation(
                    t1, t1, AF.Exp, scale=0.5), inc=sA)
                prev_nm = P.emit("vector", [tq], lambda e, h=h, t1=t1:
                                 e.tensor_scalar_mul(negm[:, h:h + 1], t1, -SCALE),
                                 inc=sV)
        es_xlnT.close()
        barrier()

        es_attnT = contextlib.ExitStack()
        attnT = sbuf(es_attnT, "attnT", [128, H * 1024], BF16, "left")
        with contextlib.ExitStack() as st2:
            expT = sbuf(st2, "expT", [128, 16 * 1024], BF16)
            kbuf = [sbuf(st2, f"kbuf{i}", [128, S], BF16) for i in range(2)]
            vbuf = [sbuf(st2, f"vbuf{i}", [128, 16 * 128], BF16) for i in range(2)]
            maskb = sbuf(st2, "maskb", [128, 16 * 128], BF16)
            dsb = sbuf(st2, "dsb", [1, 1024], F32)
            rb1 = sbuf(st2, "rb1", [128, 1024], F32)
            rb2 = sbuf(st2, "rb2", [128, 1024], F32)
            ps_s = [st2.enter_context(nc.psum_tensor(f"pss{i}", [128, 1024], F32))
                    for i in range(2)]
            ps_a = st2.enter_context(nc.psum_tensor("psa", [128, 1024], F32))
            ps_d = st2.enter_context(nc.psum_tensor("psd", [128, 1024], F32))

            mask_tok = P.emit("scalar", [], lambda e: e.dma_start(
                out=maskb[:].rearrange("p (k n) -> p k n", n=128),
                in_=masks_d.rearrange("k p n -> p k n")), inc=sC, inc_n=16)

            kv_tok = {}

            def load_kv(h, gate):
                pp = h % 2
                P.emit("sync", [gate], lambda e, h=h, pp=pp: e.dma_start(
                    out=kbuf[pp][:], in_=scr_k[h]), inc=sX, inc_n=16)
                kv_tok[h] = P.emit("sync", [], lambda e, h=h, pp=pp: e.dma_start(
                    out=vbuf[pp][:].rearrange("p (k n) -> p k n", n=128),
                    in_=scr_v[h].rearrange("k p n -> p k n")), inc=sX, inc_n=16)

            load_kv(0, None)
            load_kv(1, None)
            exp_tok = {}
            msk_tok = {}
            pv_read_tok = None
            an_tok = None
            for h in range(H):
                pp = h % 2
                s_tok = {}

                def emit_exp(h, kt, wit):
                    n = NKT[kt]
                    sp = kt % 2
                    w2 = [wit]
                    if kt == 0:
                        w2.append(pv_read_tok)
                    exp_tok[(h, kt)] = P.emit(
                        "scalar", w2, lambda e, h=h, kt=kt, sp=sp, n=n:
                        e.activation(expT[:, kt * 1024:kt * 1024 + n],
                                     ps_s[sp][:, 0:n], AF.Exp,
                                     bias=negm[:, h:h + 1], scale=SCALE),
                        inc=sA)
                    msk_tok[(h, kt)] = P.emit(
                        "vector", [exp_tok[(h, kt)]],
                        lambda e, kt=kt, n=n: e.tensor_mul(
                            expT[:, kt * 1024 + n - 128:kt * 1024 + n],
                            expT[:, kt * 1024 + n - 128:kt * 1024 + n],
                            maskb[:, kt * 128:(kt + 1) * 128]),
                        inc=sV)

                for kt in range(NT):
                    n = NKT[kt]
                    sp = kt % 2
                    w = [P.margin(kv_tok[h], sX), mask_tok,
                         exp_tok.get((h, kt - 2))]
                    nmm = (n + 511) // 512
                    tok = None
                    for c in range(nmm):
                        n0, n1 = c * 512, min(n, (c + 1) * 512)
                        tok = P.emit("tensor", w if c == 0 else [],
                                     lambda e, h=h, kt=kt, sp=sp, pp=pp,
                                     n0=n0, n1=n1: e.matmul(
                                         ps_s[sp][:, n0:n1],
                                         kbuf[pp][:, kt * 128:(kt + 1) * 128],
                                         qT[:, h * 1024 + n0:h * 1024 + n1],
                                         start=True, stop=True),
                                     inc=(sT if c == nmm - 1 else None))
                    s_tok[kt] = tok
                    if kt > 0:
                        emit_exp(h, kt - 1, tok)
                wit15 = P.emit("tensor", [], lambda e: e.matmul(
                    ps_s[1][:, 512:576], ident[:], zeros64[:],
                    start=True, stop=True), inc=sT)
                emit_exp(h, 15, wit15)
                tok = None
                for kt in range(NT):
                    n = NKT[kt]
                    w = [msk_tok[(h, kt)]]
                    if kt == 0:
                        w.append(an_tok)
                    nmm = (n + 511) // 512
                    for c in range(nmm):
                        n0, n1 = c * 512, min(n, (c + 1) * 512)
                        tok = P.emit("tensor", w if c == 0 else [],
                                     lambda e, kt=kt, pp=pp, n0=n0, n1=n1: e.matmul(
                                         ps_a[:, n0:n1],
                                         vbuf[pp][:, kt * 128:(kt + 1) * 128],
                                         expT[:, kt * 1024 + n0:kt * 1024 + n1],
                                         start=(kt == 0), stop=(kt == 15)),
                                     inc=(sT if (kt == 15 and c == nmm - 1) else None))
                pv_tok = tok
                for kt in range(NT):
                    n = NKT[kt]
                    nmm = (n + 511) // 512
                    for c in range(nmm):
                        n0, n1 = c * 512, min(n, (c + 1) * 512)
                        tok = P.emit("tensor", [t_ones] if kt == 0 else [],
                                     lambda e, kt=kt, n0=n0, n1=n1: e.matmul(
                                         ps_d[0:1, n0:n1], ones_col[:],
                                         expT[:, kt * 1024 + n0:kt * 1024 + n1],
                                         start=(kt == 0), stop=(kt == 15)),
                                     inc=(sT if (kt == 15 and c == nmm - 1)
                                          else None))
                pv_read_tok = tok
                witd = P.emit("tensor", [], lambda e: e.matmul(
                    ps_d[0:1, 0:64], ones_col[:], zeros64[:, 0:64],
                    start=False, stop=True, skip_group_check=True), inc=sT)
                pv_read_tok = witd
                dcp = P.emit("scalar", [witd], lambda e: e.copy(
                    dsb[:], ps_d[0:1, :]), inc=sA)
                bc = None
                for c in range(2):
                    bc = P.emit("tensor", [dcp] if c == 0 else [],
                                lambda e, c=c: e.matmul(
                                    ps_d[:, c * 512:(c + 1) * 512], ones_row_f[:],
                                    dsb[:, c * 512:(c + 1) * 512],
                                    start=True, stop=True),
                                inc=(sT if c == 1 else None))
                witb = P.emit("tensor", [], lambda e: e.matmul(
                    ps_d[:, 0:64], ident[:], zeros64[:],
                    start=False, stop=True, skip_group_check=True), inc=sT)
                r1 = P.emit("scalar", [witb], lambda e: e.activation(
                    rb1[:], ps_d[:], AF.Ln), inc=sA)
                r2 = P.emit("scalar", [r1], lambda e: e.activation(
                    rb2[:], rb1[:], AF.Exp, scale=-1.0), inc=sA)
                tok = None
                for c in range(2):
                    tok = P.emit("vector", [r2, pv_tok] if c == 0 else [],
                                 lambda e, h=h, c=c: e.tensor_mul(
                                     attnT[:, h * 1024 + c * 512:
                                           h * 1024 + (c + 1) * 512],
                                     ps_a[:, c * 512:(c + 1) * 512],
                                     rb2[:, c * 512:(c + 1) * 512]),
                                 inc=(sV if c == 1 else None))
                an_tok = tok
                if h + 2 < H:
                    load_kv(h + 2, an_tok)
        es_qT.close()
        barrier()

        with contextlib.ExitStack() as st3a:
            wob = [sbuf(st3a, f"wob{i}", [128, D], BF16) for i in range(WO_RING)]
            ystage = [sbuf(st3a, f"ystage{i}", [128, 2048], F32) for i in range(4)]
            ps_y = [st3a.enter_context(nc.psum_tensor(f"psy{i}", [128, 2048], F32))
                    for i in range(2)]
            wo_tok = {}
            ev_tok = {}
            ydma_tok = {}
            for h in range(WO_RING - 1):
                wo_tok[h] = P.emit("gpsimd", [], lambda e, h=h: e.dma_start(
                    out=wob[h % WO_RING][:], in_=wo[h * 128:(h + 1) * 128, :]),
                    inc=sG, inc_n=16)
            rnd = 0
            mm_rnd = {}

            def emit_y_evac(rnd2, wit):
                hb2, rr = divmod(rnd2, 16)
                j2, nh2 = divmod(rr, 2)
                sp = rnd2 % 4
                pp2 = rnd2 % 2
                ev_tok[rnd2] = P.emit(
                    "scalar", [wit, P.margin(ydma_tok.get(rnd2 - 4), sG)],
                    lambda e, sp=sp, pp2=pp2: e.copy(ystage[sp][:], ps_y[pp2][:]),
                    inc=sA)
                wprev = [ev_tok[rnd2]]
                if hb2 > 0:
                    wprev.append(ydma_tok[(hb2 - 1) * 16 + rr])
                kw = {"accum_op": ALU.add} if hb2 > 0 else {}
                ydma_tok[rnd2] = P.emit(
                    "gpsimd", wprev,
                    lambda e, j2=j2, nh2=nh2, sp=sp, kw=kw: e.dma_start(
                        out=scr_y[j2 * 128:(j2 + 1) * 128,
                                  nh2 * 2048:(nh2 + 1) * 2048],
                        in_=ystage[sp][:], **kw),
                    inc=sG, inc_n=16)

            for hb in range(4):
                for j in range(8):
                    for nh in range(2):
                        pp = rnd % 2
                        tok = None
                        for hi in range(8):
                            h = hb * 8 + hi
                            w = [P.margin(wo_tok[h], sG)]
                            if hi == 0:
                                w.append(ev_tok.get(rnd - 2))
                            for c in range(4):
                                tok = P.emit(
                                    "tensor", w if c == 0 else [],
                                    lambda e, h=h, j=j, nh=nh, c=c, pp=pp, hi=hi:
                                    e.matmul(
                                        ps_y[pp][:, c * 512:(c + 1) * 512],
                                        attnT[:, h * 1024 + j * 128:
                                              h * 1024 + (j + 1) * 128],
                                        wob[h % WO_RING][:, nh * 2048 + c * 512:
                                                         nh * 2048 + (c + 1) * 512],
                                        start=(hi == 0), stop=(hi == 7)),
                                    inc=(sT if (hi == 7 and c == 3) else None))
                        mm_rnd[rnd] = tok
                        if rnd > 0:
                            emit_y_evac(rnd - 1, tok)
                        rnd += 1
                if hb < 3:
                    for hi in range(8):
                        h = (hb + 1) * 8 + hi
                        if h < WO_RING - 1 or h >= H:
                            continue
                        wo_tok[h] = P.emit("gpsimd", [sT.tok],
                                           lambda e, h=h: e.dma_start(
                                               out=wob[h % WO_RING][:],
                                               in_=wo[h * 128:(h + 1) * 128, :]),
                                           inc=sG, inc_n=16)
            wity = P.emit("tensor", [], lambda e: e.matmul(
                ps_y[(rnd - 1) % 2][:, 0:64], ident[:], zeros64[:],
                start=False, stop=True, skip_group_check=True), inc=sT)
            emit_y_evac(63, wity)
            ydummy = P.emit("gpsimd", [ydma_tok[63]], lambda e: e.dma_start(
                out=scr_junk[0:1, 0:16], in_=ystage[3][0:1, 0:16]),
                inc=sG, inc_n=16)
            y_done = {}
            for j in range(8):
                y_done[j] = [P.margin(ydma_tok[3 * 16 + j * 2], sG),
                             P.margin(ydma_tok[3 * 16 + j * 2 + 1], sG)]
        es_attnT.close()
        barrier()

        es_x2 = contextlib.ExitStack()
        xln2T = sbuf(es_x2, "xln2T", [128, 32 * 1024], BF16, "left")
        with contextlib.ExitStack() as st3t:
            ybuf = [sbuf(st3t, f"ybuf{i}", [128, D], F32) for i in range(2)]
            xobuf = [sbuf(st3t, f"xobuf{i}", [128, D], F32) for i in range(2)]
            xatt_t = [sbuf(st3t, f"xatt{i}", [128, D], F32) for i in range(2)]
            xl2bf = [sbuf(st3t, f"xl2bf{i}", [128, D], BF16) for i in range(2)]
            sqscr = sbuf(st3t, "sqscr3t", [128, D], BF16)
            trp2 = [st3t.enter_context(nc.psum_tensor(f"trp2{i}", [128, 1024], BF16))
                    for i in range(2)]
            trpj = st3t.enter_context(nc.psum_tensor("trpj", [128, 128], BF16))
            buf_free = [None, None]
            cp2_done = {}
            xa_dma_tok = {}
            for j in range(8):
                pp = j % 2
                y_tok = P.emit("sync", y_done[j] + (buf_free[pp] or []),
                               lambda e, j=j, pp=pp: e.dma_start(
                                   out=ybuf[pp][:],
                                   in_=scr_y[j * 128:(j + 1) * 128, :]),
                               inc=sX, inc_n=16)
                xo_tok = P.emit("sync", [], lambda e, j=j, pp=pp: e.dma_start(
                    out=xobuf[pp][:], in_=x_own[j * 128:(j + 1) * 128, :]),
                    inc=sX, inc_n=16)
                tr, mn, rstd = ln_stats(ybuf[pp][:], [y_tok], pp * 6, sqscr)
                t1 = P.emit("vector", [tr], lambda e, pp=pp, mn=mn, rstd=rstd:
                            e.tensor_scalar(ybuf[pp][:], ybuf[pp][:], mn, rstd,
                                            ALU.add, ALU.mult), inc=sV)
                t2 = P.emit("vector", [t1, xo_tok], lambda e, pp=pp: e.tensor_add(
                    xatt_t[pp][:], ybuf[pp][:], xobuf[pp][:]), inc=sV)
                xa_dma = P.emit("sync", [t2], lambda e, j=j, pp=pp: e.dma_start(
                    out=scr_xatt[j * 128:(j + 1) * 128, :], in_=xatt_t[pp][:]),
                    inc=sX, inc_n=16)
                xa_dma_tok[j] = xa_dma
                tr2, mn2, rstd2 = ln_stats(xatt_t[pp][:], [t2], 12 + pp * 6, sqscr)
                t3 = P.emit("vector", [tr2], lambda e, pp=pp, mn2=mn2, rstd2=rstd2:
                            e.tensor_scalar(xl2bf[pp][:], xatt_t[pp][:], mn2, rstd2,
                                            ALU.add, ALU.mult), inc=sV)
                def emit_cp2(gidx, gp, j2, g, wit):
                    src = trp2[gp][:].rearrange("p (u n) -> p u n", n=128)
                    dst = xln2T[:].rearrange("p (a t) -> p a t", t=1024)[
                        :, g * 8:(g + 1) * 8, j2 * 128:(j2 + 1) * 128]
                    cp2_done[gidx] = P.emit(
                        "vector", [wit], lambda e, src=src, dst=dst:
                        e.tensor_copy(dst, src), inc=sV)

                tok = None
                for g in range(4):
                    gp = g % 2
                    w = [t3, cp2_done.get(j * 4 + g - 2)]
                    for u in range(8):
                        a = g * 8 + u
                        tok = P.emit("tensor", w if u == 0 else [],
                                     lambda e, pp=pp, a=a, gp=gp, u=u: e.transpose(
                                         trp2[gp][:, u * 128:(u + 1) * 128],
                                         xl2bf[pp][:, a * 128:(a + 1) * 128],
                                         ident[:]),
                                     inc=(sT if u == 7 else None))
                    if g > 0:
                        emit_cp2(j * 4 + g - 1, (g - 1) % 2, j, g - 1, tok)
                dummy2 = P.emit("tensor", [], lambda e, pp=pp: e.transpose(
                    trpj[:], xl2bf[pp][:, 0:128], ident[:]), inc=sT)
                emit_cp2(j * 4 + 3, 1, j, 3, dummy2)
                buf_free[pp] = [xa_dma, cp2_done[j * 4 + 3]]
        barrier()

        es_hts = contextlib.ExitStack()
        hts = sbuf(es_hts, "hts", [128, 32 * 1024], BF16, "left")
        with contextlib.ExitStack() as st3g:
            wgb = [sbuf(st3g, f"wgb{i}", [128, 32, 128], BF16) for i in range(2)]
            wub = [sbuf(st3g, f"wub{i}", [128, 32, 128], BF16) for i in range(2)]
            gs_bf = [sbuf(st3g, f"gsbf{i}", [128, 1024], BF16) for i in range(2)]
            ps_g = [st3g.enter_context(nc.psum_tensor(f"psg{i}", [128, 1024], F32))
                    for i in range(2)]
            ps_u = [st3g.enter_context(nc.psum_tensor(f"psu{i}", [128, 1024], F32))
                    for i in range(2)]
            wg_free = [None, None]
            hcp_tok = {}
            hdma_tok = {}
            sil_tok = {}
            strip_of = {}
            for (f0s, f1s) in FT_STRIPS:
                for ft in range(f0s, f1s):
                    strip_of[ft] = f0s

            def emit_hcp(ft, wit):
                f0 = strip_of[ft]
                fl = ft - f0
                pp = ft % 2
                hcp_tok[ft] = P.emit("vector",
                                     [wit, sil_tok[ft],
                                      hdma_tok.get(ft - 32) if f0 else None],
                                     lambda e, fl=fl, pp=pp: e.tensor_mul(
                                         hts[:, fl * 1024:(fl + 1) * 1024],
                                         ps_u[pp][:], gs_bf[pp][:]),
                                     inc=sV)
                wg_free[pp] = hcp_tok[ft]
                hdma_tok[ft] = P.emit("sync", [hcp_tok[ft]],
                                      lambda e, ft=ft, fl=fl: e.dma_start(
                                          out=scr_h[ft],
                                          in_=hts[:, fl * 1024:(fl + 1) * 1024]),
                                      inc=sX, inc_n=16)

            for ft in range(NFT):
                pp = ft % 2
                P.emit("gpsimd", [wg_free[pp]], lambda e, ft=ft, pp=pp:
                       e.dma_start(out=wgb[pp][:],
                                   in_=wgT[:, ft * 128:(ft + 1) * 128].rearrange(
                                       "(a p) n -> p a n", p=128)),
                       inc=sG, inc_n=16)
                wg_tok = P.emit("gpsimd", [], lambda e, ft=ft, pp=pp:
                                e.dma_start(out=wub[pp][:],
                                            in_=wuT[:, ft * 128:(ft + 1) * 128]
                                            .rearrange("(a p) n -> p a n", p=128)),
                                inc=sG, inc_n=16)
                w = [P.margin(wg_tok, sG), hcp_tok.get(ft - 2)]
                tokg = None
                for dk in range(32):
                    for c in range(2):
                        tokg = P.emit(
                            "tensor", w if (dk == 0 and c == 0) else [],
                            lambda e, dk=dk, c=c, pp=pp: e.matmul(
                                ps_g[pp][:, c * 512:(c + 1) * 512],
                                wgb[pp][:, dk, :],
                                xln2T[:, dk * 1024 + c * 512:
                                      dk * 1024 + (c + 1) * 512],
                                start=(dk == 0), stop=(dk == 31)),
                            inc=(sT if (dk == 31 and c == 1) else None))
                if ft > 0:
                    emit_hcp(ft - 1, tokg)
                toku = None
                for dk in range(32):
                    for c in range(2):
                        toku = P.emit(
                            "tensor", [],
                            lambda e, dk=dk, c=c, pp=pp: e.matmul(
                                ps_u[pp][:, c * 512:(c + 1) * 512],
                                wub[pp][:, dk, :],
                                xln2T[:, dk * 1024 + c * 512:
                                      dk * 1024 + (c + 1) * 512],
                                start=(dk == 0), stop=(dk == 31)),
                            inc=(sT if (dk == 31 and c == 1) else None))
                sil_tok[ft] = P.emit("scalar", [toku, hcp_tok.get(ft - 2)],
                                     lambda e, pp=pp: e.activation(
                                         gs_bf[pp][:], ps_g[pp][:], AF.Silu),
                                     inc=sA)
            witu = P.emit("tensor", [], lambda e: e.matmul(
                ps_u[(NFT - 1) % 2][:, 0:64], ident[:], zeros64[:],
                start=False, stop=True, skip_group_check=True), inc=sT)
            emit_hcp(NFT - 1, witu)
            h_spill_done = sX.tok
        es_hts.close()
        es_x2.close()
        barrier()

        with contextlib.ExitStack() as st3d:
            hfull = sbuf(st3d, "hfull", [128, NFT * 1024], BF16)
            wdb = [sbuf(st3d, f"wdb{i}", [128, 512], BF16) for i in range(6)]
            y2st = [sbuf(st3d, f"y2st{i}", [128, 512], F32) for i in range(8)]
            ps_dn = [st3d.enter_context(
                nc.psum_tensor(f"psdn{i}", [128, 512], F32)) for i in range(8)]
            hh_tok = None
            for hc in range(8):
                f0h, f1h = hc * 11, min(NFT, (hc + 1) * 11)
                hh_tok = P.emit("scalar",
                                [h_spill_done] if hc == 0 else [],
                                lambda e, f0h=f0h, f1h=f1h: e.dma_start(
                                    out=hfull[:].rearrange(
                                        "p (f n) -> p f n", n=1024)[:, f0h:f1h, :],
                                    in_=scr_h[f0h:f1h].rearrange("f p n -> p f n")),
                                inc=sC, inc_n=16)
            ft_mm = {}
            ev_last = {}
            y2dma = {}
            for dc in range(8):
                for ft in range(NFT):
                    it = dc * NFT + ft
                    wp = it % 6
                    wd_tok = P.emit("gpsimd", [ft_mm.get(it - 6)],
                                    lambda e, dc=dc, ft=ft, wp=wp: e.dma_start(
                                        out=wdb[wp][:],
                                        in_=wdT[ft * 128:(ft + 1) * 128,
                                                dc * 512:(dc + 1) * 512]),
                                    inc=sG, inc_n=16)
                    w = [P.margin(wd_tok, sG), hh_tok]
                    if ft == 0:
                        w.append(ev_last.get(dc - 1))
                    tok = None
                    for qt in range(8):
                        tok = P.emit(
                            "tensor", w if qt == 0 else [],
                            lambda e, ft=ft, qt=qt, wp=wp: e.matmul(
                                ps_dn[qt][:],
                                hfull[:, ft * 1024 + qt * 128:
                                      ft * 1024 + (qt + 1) * 128],
                                wdb[wp][:],
                                start=(ft == 0), stop=(ft == NFT - 1)),
                            inc=(sT if qt == 7 else None))
                    ft_mm[it] = tok
                witdn = P.emit("tensor", [], lambda e: e.matmul(
                    ps_dn[7][:, 0:64], ident[:], zeros64[:],
                    start=False, stop=True, skip_group_check=True), inc=sT)
                last = None
                for qt in range(8):
                    evt = P.emit("scalar",
                                 [witdn, y2dma.get((dc - 1, qt))],
                                 lambda e, qt=qt: e.copy(
                                     y2st[qt][:], ps_dn[qt][:]),
                                 inc=sA)
                    last = P.emit("sync", [evt],
                                  lambda e, dc=dc, qt=qt: e.dma_start(
                                      out=scr_y2[qt * 128:(qt + 1) * 128,
                                                 dc * 512:(dc + 1) * 512],
                                      in_=y2st[qt][:]),
                                  inc=sX, inc_n=16)
                    y2dma[(dc, qt)] = last
                ev_last[dc] = evt
            P.emit("sync", [y2dma[(7, 7)]], lambda e: e.dma_start(
                out=scr_junk[0:1, 16:32], in_=y2st[7][0:1, 0:16]),
                inc=sX, inc_n=16)
            y2_done = sX.tok
        barrier()

        with contextlib.ExitStack() as st3f:
            y2buf = [sbuf(st3f, f"y2buf{i}", [128, D], F32) for i in range(2)]
            xabuf = [sbuf(st3f, f"xabuf{i}", [128, D], F32) for i in range(2)]
            obuf = [sbuf(st3f, f"obuf{i}", [128, D], F32) for i in range(2)]
            sqscr = sbuf(st3f, "sqscr3f", [128, D], BF16)
            odma = [None, None]
            for j in range(8):
                pp = j % 2
                y2_tok = P.emit("scalar", [y2_done, odma[pp]],
                                lambda e, j=j, pp=pp: e.dma_start(
                                    out=y2buf[pp][:],
                                    in_=scr_y2[j * 128:(j + 1) * 128, :]),
                                inc=sC, inc_n=16)
                xa_tok = P.emit("scalar", [P.margin(xa_dma_tok[j], sX)],
                                lambda e, j=j, pp=pp: e.dma_start(
                                    out=xabuf[pp][:],
                                    in_=scr_xatt[j * 128:(j + 1) * 128, :]),
                                inc=sC, inc_n=16)
                tr, mn, rstd = ln_stats(y2buf[pp][:], [y2_tok], pp * 6, sqscr)
                t1 = P.emit("vector", [tr], lambda e, pp=pp, mn=mn, rstd=rstd:
                            e.tensor_scalar(y2buf[pp][:], y2buf[pp][:], mn, rstd,
                                            ALU.add, ALU.mult), inc=sV)
                t2 = P.emit("vector", [t1, xa_tok], lambda e, pp=pp: e.tensor_add(
                    obuf[pp][:], y2buf[pp][:], xabuf[pp][:]), inc=sV)
                odma[pp] = P.emit("sync", [t2], lambda e, j=j, pp=pp: e.dma_start(
                    out=out_d[j * 128:(j + 1) * 128, :], in_=obuf[pp][:]),
                    inc=sX, inc_n=16)
            P.raw_wait("sync", sX.tok)

        with nc.Block() as block:
            block.tensor(lambda e: [f(e) for f in P.q["tensor"]])
            block.vector(lambda e: [f(e) for f in P.q["vector"]])
            block.scalar(lambda e: [f(e) for f in P.q["scalar"]])
            block.gpsimd(lambda e: [f(e) for f in P.q["gpsimd"]])
            block.sync(lambda e: [f(e) for f in P.q["sync"]])

    return nc


def kernel(L0_attn_hidden_states, L0_attn_wq, L0_attn_wkv, L0_attn_wo,
           L0_ffn_W_gate, L0_ffn_W_up, L0_ffn_W_down):
    x = np.asarray(L0_attn_hidden_states, dtype=np.float32)
    wq = np.asarray(L0_attn_wq, dtype=np.float32)
    wkv = np.asarray(L0_attn_wkv, dtype=np.float32)
    wo = np.asarray(L0_attn_wo, dtype=np.float32)
    wg = np.asarray(L0_ffn_W_gate, dtype=np.float32)
    wu = np.asarray(L0_ffn_W_up, dtype=np.float32)
    wd = np.asarray(L0_ffn_W_down, dtype=np.float32)

    wq_u = np.ascontiguousarray(wq.reshape(D, H, 192)[:, :, :128].reshape(D, 4096))
    wk_u = np.ascontiguousarray(wkv[:, :H * DN])
    wv_u = np.ascontiguousarray(wkv[:, H * DN:H * (DN + DV)])
    wgT = np.ascontiguousarray(wg.T)
    wuT = np.ascontiguousarray(wu.T)
    wdT = np.ascontiguousarray(wd.T)

    in_maps = []
    rows_by_core = []
    for c in range(8):
        b, par = c // 2, c % 2
        tiles_true, perm = core_layout(par)
        own_rows = np.concatenate(
            [np.arange(t * 128, (t + 1) * 128) for t in tiles_true])
        xb = x[b]
        in_maps.append({
            "x_own": np.ascontiguousarray(xb[own_rows]),
            "x_full": np.ascontiguousarray(
                np.concatenate([xb[t * 128:(t + 1) * 128] for t in perm], axis=0)),
            "wq": wq_u, "wk": wk_u, "wv": wv_u, "wo": wo,
            "wgT": wgT, "wuT": wuT, "wdT": wdT,
            "masks": build_masks(par),
        })
        rows_by_core.append((b, own_rows))

    nc = build_nc()
    res = run_bass_kernel_spmd(nc, in_maps, core_ids=list(range(8)))

    out = np.zeros((NB, S, D), dtype=np.float32)
    for c in range(8):
        b, own_rows = rows_by_core[c]
        out[b][own_rows] = res.results[c]["out"]
    return out


# version 07
# speedup vs baseline: 1.0555x; 1.0555x over previous
import contextlib
import math
import numpy as np
import ml_dtypes
import concourse.bass as bass
import concourse.mybir as mybir
from concourse.bass_utils import run_bass_kernel_spmd

F32 = mybir.dt.float32
BF16 = mybir.dt.bfloat16
AF = mybir.ActivationFunctionType
ALU = mybir.AluOpType
AX = mybir.AxisListType
BFNP = ml_dtypes.bfloat16

H, DN, DV, D, DFF, NB, S = 32, 128, 128, 4096, 11008, 4, 2048
SCALE = 1.0 / math.sqrt(192.0)
EPS = 1e-5
NT = 16
OWN_KT = [15, 12, 11, 8, 7, 4, 3, 0]
EXT = [16 - 2 * j for j in range(8)]
NKT = [128 * math.ceil((16 - kt) / 2) for kt in range(16)]
NFT = DFF // 128
EXP_GROUPS = [[0], [1], [2], [3], [4], [5], [6], [7], [8, 9], [10, 15],
              [11, 14], [12, 13]]
OFF = [0] * 17
_off = 0
for _g in EXP_GROUPS:
    for _kt in _g:
        OFF[_kt] = _off
        _off += NKT[_kt]
NEXPC = _off
FT_STRIPS = [(0, 32), (32, 64), (64, 86)]
WO_RING = 9


def core_layout(par):
    if par == 0:
        tiles_true = [15, 12, 11, 8, 7, 4, 3, 0]
        perm = list(range(16))
    else:
        tiles_true = [14, 13, 10, 9, 6, 5, 2, 1]
        perm = []
        for g in range(4):
            perm += [4 * g + 1, 4 * g + 0, 4 * g + 3, 4 * g + 2]
    return tiles_true, perm


def build_masks(par):
    tiles_true, perm = core_layout(par)
    masks = np.zeros((16, 128, 128), dtype=np.float32)
    for kt in range(16):
        j = (15 - kt) // 2
        qpos = tiles_true[j] * 128 + np.arange(128)
        kpos = perm[kt] * 128 + np.arange(128)
        masks[kt] = (kpos[:, None] <= qpos[None, :]).astype(np.float32)
    for j in range(8):
        qpos = tiles_true[j] * 128 + np.arange(128)
        cov = np.zeros((2048, 128))
        for kt in range(EXT[j]):
            kpos = perm[kt] * 128 + np.arange(128)
            w = masks[kt] if kt in (EXT[j] - 2, EXT[j] - 1) else np.ones((128, 128))
            cov[kpos] += w
        want = (np.arange(2048)[:, None] <= qpos[None, :]).astype(np.float32)
        assert np.array_equal(cov, want), f"mask coverage par={par} slot={j}"
    return masks.astype(BFNP)


class Ctr:
    def __init__(self, handle):
        self.h = handle
        self.v = 0

    @property
    def tok(self):
        return (self.h, self.v)


class Prog:
    ENGINES = ("tensor", "vector", "scalar", "gpsimd", "sync")

    def __init__(self):
        self.q = {e: [] for e in self.ENGINES}
        self.waited = {e: {} for e in self.ENGINES}

    def emit(self, eng, waits, build, inc=None, inc_n=1):
        ws = []
        seen = self.waited[eng]
        for t in waits:
            if t is None:
                continue
            h, v = t
            if v <= 0:
                continue
            key = id(h)
            if seen.get(key, -1) >= v:
                continue
            seen[key] = v
            ws.append((h, v))
        tok = None
        if inc is not None:
            inc.v += inc_n
            tok = (inc.h, inc.v)
        ih = inc.h if inc is not None else None

        def fn(e, ws=tuple(ws), build=build, ih=ih, iv=inc_n):
            for (h, v) in ws:
                e.wait_ge(h, v)
            ins = build(e)
            if ih is not None:
                ins.then_inc(ih, iv)
        self.q[eng].append(fn)
        return tok

    def margin(self, tok, ctr):
        if tok is None or ctr.v < tok[1] + 16:
            return tok
        return (tok[0], tok[1] + 16)

    def raw_wait(self, eng, tok):
        h, v = tok
        if self.waited[eng].get(id(h), -1) >= v:
            return
        self.waited[eng][id(h)] = v
        self.q[eng].append(lambda e, h=h, v=v: e.wait_ge(h, v))


def build_nc():
    nc = bass.Bass("TRN2", target_bir_lowering=False)

    x_own = nc.dram_tensor("x_own", [1024, D], F32, kind="ExternalInput").ap()
    x_full = nc.dram_tensor("x_full", [S, D], F32, kind="ExternalInput").ap()
    wq = nc.dram_tensor("wq", [H, D, 128], F32, kind="ExternalInput").ap()
    wk = nc.dram_tensor("wk", [H, D, 128], F32, kind="ExternalInput").ap()
    wv = nc.dram_tensor("wv", [H, D, 128], F32, kind="ExternalInput").ap()
    wo = nc.dram_tensor("wo", [4096, D], F32, kind="ExternalInput").ap()
    wgT = nc.dram_tensor("wgT", [NFT, D, 128], F32, kind="ExternalInput").ap()
    wuT = nc.dram_tensor("wuT", [NFT, D, 128], F32, kind="ExternalInput").ap()
    wdT = nc.dram_tensor("wdT", [8, DFF, 512], F32, kind="ExternalInput").ap()
    masks_d = nc.dram_tensor("masks", [16, 128, 128], BF16, kind="ExternalInput").ap()
    out_d = nc.dram_tensor("out", [1024, D], F32, kind="ExternalOutput").ap()

    scr_k = nc.dram_tensor("scr_k", [H, 128, S], BF16, kind="Internal").ap()
    scr_v = nc.dram_tensor("scr_v", [H, 16, 128, 128], BF16, kind="Internal").ap()
    scr_y = nc.dram_tensor("scr_y", [1024, D], F32, kind="Internal").ap()
    scr_xatt = nc.dram_tensor("scr_xatt", [1024, D], F32, kind="Internal").ap()
    scr_h = nc.dram_tensor("scr_h", [NFT, 128, 1024], BF16, kind="Internal").ap()
    scr_y2 = nc.dram_tensor("scr_y2", [1024, D], F32, kind="Internal").ap()
    scr_junk = nc.dram_tensor("scr_junk", [128, 64], F32, kind="Internal").ap()

    ident_d = nc.inline_tensor(np.eye(128, dtype=BFNP).view(np.uint16), name="ident")

    P = Prog()
    ENG5 = Prog.ENGINES

    with contextlib.ExitStack() as es:
        def sbuf(stack, name, shape, dtype, side="right"):
            return stack.enter_context(nc.sbuf_tensor(name, shape, dtype, side=side))

        ident = sbuf(es, "ident_sb", [128, 128], BF16, "left")
        ones_col = sbuf(es, "ones_col", [128, 1], BF16, "left")
        ones_row = sbuf(es, "ones_row", [1, 128], BF16, "left")
        ones_row_f = sbuf(es, "ones_row_f", [1, 128], F32, "left")
        ones_sq = sbuf(es, "ones_sq", [128, 128], BF16, "left")
        stats = sbuf(es, "stats", [128, 32], F32, "left")
        negm = sbuf(es, "negm", [128, H], F32, "left")
        eps_t = sbuf(es, "eps_t", [128, 1], F32, "left")
        tiny_t = sbuf(es, "tiny_t", [128, 1], F32, "left")
        nrm_acc = sbuf(es, "nrm_acc", [128, 2 * H], F32, "left")
        zeros64 = sbuf(es, "zeros64", [128, 64], BF16, "left")

        sG = Ctr(es.enter_context(nc.semaphore("sG")))
        sX = Ctr(es.enter_context(nc.semaphore("sX")))
        sC = Ctr(es.enter_context(nc.semaphore("sC")))
        sT = Ctr(es.enter_context(nc.semaphore("sT")))
        sV = Ctr(es.enter_context(nc.semaphore("sV")))
        sA = Ctr(es.enter_context(nc.semaphore("sA")))
        sB = Ctr(es.enter_context(nc.semaphore("sB")))

        def barrier():
            sB.v += 5
            tgt = sB.v
            for e in ENG5:
                P.emit(e, [], lambda en: en.sem_inc(sB.h, 1))
            for e in ENG5:
                P.raw_wait(e, (sB.h, tgt))

        def ln_stats(xin_ap, ready_toks, c0, sqscr):
            ssum = stats[:, c0:c0 + 1]
            mn = stats[:, c0 + 1:c0 + 2]
            msq = stats[:, c0 + 2:c0 + 3]
            varp = stats[:, c0 + 3:c0 + 4]
            rstd = stats[:, c0 + 4:c0 + 5]
            ssq = stats[:, c0 + 5:c0 + 6]
            fd = xin_ap.shape[1]
            t0 = P.emit("vector", ready_toks, lambda e: e.reduce_sum(
                ssum, xin_ap, axis=AX.X), inc=sV)
            t1 = P.emit("vector", [t0], lambda e: e.tensor_scalar_mul(
                mn, ssum, -1.0 / fd), inc=sV)
            tv = P.emit("vector", [t1], lambda e: e.tensor_mul(msq, mn, mn), inc=sV)
            ta = P.emit("scalar", ready_toks, lambda e: e.activation(
                sqscr[:, 0:fd], xin_ap, AF.Square, accum_out=ssq), inc=sA)
            tv2 = P.emit("vector", [tv, ta], lambda e: e.scalar_tensor_tensor(
                varp, ssq, 1.0 / fd, msq, ALU.mult, ALU.subtract), inc=sV)
            tl = P.emit("scalar", [tv2], lambda e: e.activation(
                rstd, varp, AF.Ln, bias=eps_t[:, 0:1]), inc=sA)
            tr = P.emit("scalar", [tl], lambda e: e.activation(
                rstd, rstd, AF.Exp, scale=-0.5), inc=sA)
            return tr, mn, rstd

        t_id = P.emit("sync", [], lambda e: e.dma_start(
            out=ident[:], in_=ident_d.ap().bitcast(BF16)), inc=sX, inc_n=16)
        P.emit("vector", [], lambda e: e.memset(ones_col[:], 1.0))
        P.emit("vector", [], lambda e: e.memset(ones_row[:], 1.0))
        P.emit("vector", [], lambda e: e.memset(ones_row_f[:], 1.0))
        P.emit("vector", [], lambda e: e.memset(eps_t[:], EPS))
        P.emit("vector", [], lambda e: e.memset(tiny_t[:], 1e-30))
        P.emit("vector", [], lambda e: e.memset(zeros64[:], 0.0))
        t_ones = P.emit("vector", [], lambda e: e.memset(ones_sq[:], 1.0), inc=sV)
        barrier()

        es_xlnT = contextlib.ExitStack()
        xlnT = sbuf(es_xlnT, "xlnT", [128, 32 * S], BF16, "left")

        with contextlib.ExitStack() as st1a:
            xin = [sbuf(st1a, f"xin{i}", [128, D], F32) for i in range(2)]
            xln_bf = [sbuf(st1a, f"xlnbf{i}", [128, D], BF16) for i in range(2)]
            sqscr = sbuf(st1a, "sqscr1a", [128, D], BF16)
            trps = [st1a.enter_context(nc.psum_tensor(f"trps{i}", [128, 1024], BF16))
                    for i in range(2)]
            trjunk = st1a.enter_context(nc.psum_tensor("trjunk", [128, 128], BF16))
            xin_free = [None, None]
            cp_done = {}

            def emit_cp(gidx, gp, kt, g, wit):
                src = trps[gp][:].rearrange("p (u n) -> p u n", n=128)
                dst = xlnT[:].rearrange("p (a t) -> p a t", t=S)[
                    :, g * 8:(g + 1) * 8, kt * 128:(kt + 1) * 128]
                cp_done[gidx] = P.emit(
                    "vector", [wit], lambda e, src=src, dst=dst:
                    e.tensor_copy(dst, src), inc=sV)

            for kt in range(NT):
                pp = kt % 2
                xin_tok = P.emit("sync", [xin_free[pp]],
                                 lambda e, kt=kt, pp=pp: e.dma_start(
                                     out=xin[pp][:],
                                     in_=x_full[kt * 128:(kt + 1) * 128, :]),
                                 inc=sX, inc_n=16)
                tr, mn, rstd = ln_stats(xin[pp][:], [xin_tok], pp * 6, sqscr)
                xln_tok = P.emit("vector", [tr],
                                 lambda e, pp=pp, mn=mn, rstd=rstd: e.tensor_scalar(
                                     xln_bf[pp][:], xin[pp][:], mn, rstd,
                                     ALU.add, ALU.mult), inc=sV)
                xin_free[pp] = xln_tok
                gtok = {}
                for g in range(4):
                    gp = g % 2
                    w = [xln_tok, t_id, cp_done.get(kt * 4 + g - 2)]
                    tok = None
                    for u in range(8):
                        a = g * 8 + u
                        tok = P.emit("tensor", w if u == 0 else [],
                                     lambda e, pp=pp, a=a, gp=gp, u=u: e.transpose(
                                         trps[gp][:, u * 128:(u + 1) * 128],
                                         xln_bf[pp][:, a * 128:(a + 1) * 128],
                                         ident[:]),
                                     inc=(sT if u == 7 else None))
                    gtok[g] = tok
                    if g > 0:
                        emit_cp(kt * 4 + g - 1, (g - 1) % 2, kt, g - 1, tok)
                dummy = P.emit("tensor", [], lambda e, pp=pp: e.transpose(
                    trjunk[:], xln_bf[pp][:, 0:128], ident[:]), inc=sT)
                emit_cp(kt * 4 + 3, 1, kt, 3, dummy)
            xlnT_ready = sV.tok
        barrier()

        with contextlib.ExitStack() as st1b:
            wvb = [sbuf(st1b, f"wvb{i}", [128, 32, 512], BF16) for i in range(2)]
            vstage = [sbuf(st1b, f"vstage{i}", [128, 512], BF16) for i in range(2)]
            ps_v = [st1b.enter_context(nc.psum_tensor(f"psv{i}", [128, 512], F32))
                    for i in range(2)]
            wv_tok = [None, None]
            wv_free = [None, None]
            mm_done = {}
            ev_done = {}
            st_done = {}

            def emit_v_evac(it, wit):
                vp = it % 2
                n, kt = it // 16, it % 16
                ev_done[it] = P.emit("scalar", [wit, st_done.get(it - 2)],
                                     lambda e, vp=vp: e.copy(
                                         vstage[vp][:], ps_v[vp][:]), inc=sA)
                last = None
                for qh in range(4):
                    last = P.emit("sync", [ev_done[it]] if qh == 0 else [],
                                  lambda e, n=n, kt=kt, qh=qh, vp=vp: e.dma_start(
                                      out=scr_v[n * 4 + qh, kt],
                                      in_=vstage[vp][:, qh * 128:(qh + 1) * 128]),
                                  inc=sX, inc_n=16)
                st_done[it] = last

            for n in range(8):
                pp = n % 2
                for qh in range(4):
                    wv_tok[pp] = P.emit("gpsimd", [wv_free[pp]] if qh == 0 else [],
                                        lambda e, n=n, pp=pp, qh=qh: e.dma_start(
                                            out=wvb[pp][:, :, qh * 128:(qh + 1) * 128],
                                            in_=wv[n * 4 + qh].rearrange(
                                                "(a p) n -> p a n", p=128)),
                                        inc=sG, inc_n=16)
                for kt in range(NT):
                    it = n * 16 + kt
                    vp = it % 2
                    w = [P.margin(wv_tok[pp], sG), xlnT_ready, ev_done.get(it - 2)]
                    tok = None
                    for dk in range(32):
                        tok = P.emit("tensor", w if dk == 0 else [],
                                     lambda e, dk=dk, kt=kt, pp=pp, vp=vp: e.matmul(
                                         ps_v[vp][:],
                                         xlnT[:, dk * S + kt * 128:
                                              dk * S + (kt + 1) * 128],
                                         wvb[pp][:, dk, :],
                                         start=(dk == 0), stop=(dk == 31)),
                                     inc=(sT if dk == 31 else None))
                    mm_done[it] = tok
                    if it > 0:
                        emit_v_evac(it - 1, tok)
                wv_free[pp] = mm_done[n * 16 + 15]
            wit = P.emit("tensor", [], lambda e: e.matmul(
                ps_v[1][:, 0:64], ident[:], zeros64[:], start=False, stop=True, skip_group_check=True),
                inc=sT)
            emit_v_evac(127, wit)
        barrier()

        with contextlib.ExitStack() as st1c:
            wkb = [sbuf(st1c, f"wkb{i}", [128, 32, 128], BF16) for i in range(2)]
            kstage = [sbuf(st1c, f"kstage{i}", [128, S], BF16) for i in range(2)]
            ksq = sbuf(st1c, "ksq", [128, S], BF16)
            ps_k = st1c.enter_context(nc.psum_tensor("psk", [128, S], F32))
            ps_n = st1c.enter_context(nc.psum_tensor("psn", [128, 2048], F32))
            wk_tok = [None, None]
            wk_free = [None, None]
            ev_tok = None
            ev_hist = {}
            st_tok = None
            prev_nm = None
            for h in range(H):
                ev_prev = ev_hist.get(h - 2)
                pp = h % 2
                wk_tok[pp] = P.emit("gpsimd", [wk_free[pp]],
                                    lambda e, h=h, pp=pp: e.dma_start(
                                        out=wkb[pp][:],
                                        in_=wk[h].rearrange(
                                            "(a p) n -> p a n", p=128)),
                                    inc=sG, inc_n=16)
                w = [P.margin(wk_tok[pp], sG), ev_prev]
                tok = None
                for dk in range(32):
                    for c in range(4):
                        tok = P.emit("tensor", w if (dk == 0 and c == 0) else [],
                                     lambda e, dk=dk, c=c, pp=pp: e.matmul(
                                         ps_k[:, c * 512:(c + 1) * 512],
                                         wkb[pp][:, dk, :],
                                         xlnT[:, dk * S + c * 512:
                                              dk * S + (c + 1) * 512],
                                         start=(dk == 0), stop=(dk == 31)),
                                     inc=(sT if (dk == 31 and c == 3) else None))
                wk_free[pp] = tok
                wit = P.emit("tensor", [], lambda e: e.matmul(
                    ps_k[:, 0:64], ident[:], zeros64[:], start=False, stop=True, skip_group_check=True),
                    inc=sT)
                ks = h % 2
                ev_tok = P.emit("scalar", [wit, st_tok], lambda e, ks=ks: e.copy(
                    kstage[ks][:], ps_k[:]), inc=sA)
                st_tok = P.emit("sync", [ev_tok], lambda e, h=h, ks=ks: e.dma_start(
                    out=scr_k[h], in_=kstage[ks][:]), inc=sX, inc_n=16)
                ev_hist[h] = ev_tok
                sq_tok = P.emit("vector", [ev_tok, prev_nm],
                                lambda e, ks=ks: e.tensor_mul(
                                    ksq[:], kstage[ks][:], kstage[ks][:]),
                                inc=sV)
                km = nrm_acc[:, H + h:H + h + 1]
                red = stats[:, 24:25]
                w_nm = [sq_tok, prev_nm]
                for c in range(4):
                    P.emit("tensor", w_nm if c == 0 else [],
                           lambda e, c=c: e.matmul(
                               ps_n[:, c * 512:(c + 1) * 512], ones_sq[:],
                               ksq[:, c * 512:(c + 1) * 512],
                               start=True, stop=True))
                witn = P.emit("tensor", [], lambda e: e.matmul(
                    ps_n[:, 0:64], ident[:], zeros64[:],
                    start=False, stop=True, skip_group_check=True), inc=sT)
                rm = P.emit("vector", [witn], lambda e, red=red:
                            e.reduce_max(red, ps_n[:], axis=AX.X), inc=sV)
                prev_nm = P.emit("vector", [rm], lambda e, km=km, red=red:
                                 e.tensor_copy(km, red), inc=sV)
        barrier()

        es_qT = contextlib.ExitStack()
        qT = sbuf(es_qT, "qT", [128, H * 1024], BF16)
        with contextlib.ExitStack() as st1d:
            wqb = [sbuf(st1d, f"wqb{i}", [128, 16, 128], BF16) for i in range(2)]
            qsq = sbuf(st1d, "qsq", [128, 1024], BF16)
            ps_q = [st1d.enter_context(nc.psum_tensor(f"psq{i}", [128, 1024], F32))
                    for i in range(2)]
            ps_n2 = st1d.enter_context(nc.psum_tensor("psn2", [128, 1024], F32))
            pcol = []
            for j in range(8):
                kt = OWN_KT[j]
                pcol.append((kt // 4) * 128 if kt % 4 == 0 else 512 + (kt // 4) * 128)
            wq_tok = {}
            wq_free = {}
            cp_tok = {}
            prev_nm = None
            for h in range(H):
                qp = h % 2
                for dh in range(2):
                    ch = h * 2 + dh
                    bp = ch % 2
                    wq_tok[ch] = P.emit("gpsimd", [wq_free.get(ch - 2)],
                                        lambda e, h=h, dh=dh, bp=bp: e.dma_start(
                                            out=wqb[bp][:],
                                            in_=wq[h, dh * 2048:(dh + 1) * 2048, :]
                                            .rearrange("(a p) n -> p a n", p=128)),
                                        inc=sG, inc_n=16)
                tok = None
                for dh in range(2):
                    ch = h * 2 + dh
                    bp = ch % 2
                    w = [P.margin(wq_tok[ch], sG), cp_tok.get(h - 2)]
                    for dk in range(16):
                        dkg = dh * 16 + dk
                        rhs_a = xlnT[:, dkg * S:(dkg + 1) * S].rearrange(
                            "p (k n) -> p k n", n=128)
                        for half in range(2):
                            sel = rhs_a[:, (0 if half == 0 else 3)::4, :]
                            tok = P.emit(
                                "tensor", w if (dk == 0 and half == 0) else [],
                                lambda e, dkg=dkg, half=half, qp=qp, bp=bp, sel=sel:
                                e.matmul(ps_q[qp][:, half * 512:(half + 1) * 512],
                                         wqb[bp][:, dkg % 16, :], sel,
                                         start=(dkg == 0), stop=(dkg == 31)),
                                inc=(sT if (dkg == 31 and half == 1) else None))
                    wq_free[ch] = tok if dh == 1 else None
                mm_tok = tok
                wq_free[h * 2] = mm_tok
                wq_free[h * 2 + 1] = mm_tok
                wit = P.emit("tensor", [], lambda e, qp=qp: e.matmul(
                    ps_q[qp][:, 0:64], ident[:], zeros64[:],
                    start=False, stop=True, skip_group_check=True), inc=sT)
                last = None
                for j in range(8):
                    last = P.emit("vector", [wit] if j == 0 else [],
                                  lambda e, j=j, h=h, qp=qp: e.tensor_copy(
                                      qT[:, h * 1024 + j * 128:
                                         h * 1024 + (j + 1) * 128],
                                      ps_q[qp][:, pcol[j]:pcol[j] + 128]),
                                  inc=(sV if j == 7 else None))
                cp_tok[h] = last
                sq_tok = P.emit("vector", [last, prev_nm],
                                lambda e, h=h: e.tensor_mul(
                                    qsq[:], qT[:, h * 1024:(h + 1) * 1024],
                                    qT[:, h * 1024:(h + 1) * 1024]), inc=sV)
                qm = nrm_acc[:, h:h + 1]
                red = stats[:, 25:26]
                w_nm = [sq_tok, prev_nm]
                for c in range(2):
                    P.emit("tensor", w_nm if c == 0 else [],
                           lambda e, c=c: e.matmul(
                               ps_n2[:, c * 512:(c + 1) * 512], ones_sq[:],
                               qsq[:, c * 512:(c + 1) * 512],
                               start=True, stop=True))
                witn = P.emit("tensor", [], lambda e: e.matmul(
                    ps_n2[:, 0:64], ident[:], zeros64[:],
                    start=False, stop=True, skip_group_check=True), inc=sT)
                rm = P.emit("vector", [witn], lambda e, red=red:
                            e.reduce_max(red, ps_n2[:], axis=AX.X), inc=sV)
                gate = P.emit("vector", [rm], lambda e, qm=qm, red=red:
                              e.tensor_copy(qm, red), inc=sV)
                prev_nm = gate
                t1 = stats[:, 26:27]
                tmul = P.emit("vector", [gate], lambda e, qm=qm, h=h, t1=t1:
                              e.tensor_mul(t1, qm, nrm_acc[:, H + h:H + h + 1]),
                              inc=sV)
                tln = P.emit("scalar", [tmul], lambda e, t1=t1: e.activation(
                    t1, t1, AF.Ln, bias=tiny_t[:, 0:1]), inc=sA)
                tq = P.emit("scalar", [tln], lambda e, t1=t1: e.activation(
                    t1, t1, AF.Exp, scale=0.5), inc=sA)
                prev_nm = P.emit("vector", [tq], lambda e, h=h, t1=t1:
                                 e.tensor_scalar_mul(negm[:, h:h + 1], t1, -SCALE),
                                 inc=sV)
        es_xlnT.close()
        barrier()

        es_attnT = contextlib.ExitStack()
        attnT = sbuf(es_attnT, "attnT", [128, H * 1024], BF16, "left")
        with contextlib.ExitStack() as st2:
            expT = sbuf(st2, "expT", [128, NEXPC], BF16)
            kbuf = [sbuf(st2, f"kbuf{i}", [128, S], BF16) for i in range(2)]
            vbuf = [sbuf(st2, f"vbuf{i}", [128, 16 * 128], BF16) for i in range(2)]
            maskb = sbuf(st2, "maskb", [128, 16 * 128], BF16)
            dsb = sbuf(st2, "dsb", [1, 1024], F32)
            rb1 = sbuf(st2, "rb1", [128, 1024], F32)
            rb2 = sbuf(st2, "rb2", [128, 1024], F32)
            ps_s = [st2.enter_context(nc.psum_tensor(f"pss{i}", [128, 1024], F32))
                    for i in range(3)]
            ps_a = st2.enter_context(nc.psum_tensor("psa", [128, 1024], F32))

            mask_tok = P.emit("scalar", [], lambda e: e.dma_start(
                out=maskb[:].rearrange("p (k n) -> p k n", n=128),
                in_=masks_d.rearrange("k p n -> p k n")), inc=sC, inc_n=16)

            kv_tok = {}

            def load_kv(h, gate):
                pp = h % 2
                P.emit("sync", [gate], lambda e, h=h, pp=pp: e.dma_start(
                    out=kbuf[pp][:], in_=scr_k[h]), inc=sX, inc_n=16)
                kv_tok[h] = P.emit("sync", [], lambda e, h=h, pp=pp: e.dma_start(
                    out=vbuf[pp][:].rearrange("p (k n) -> p k n", n=128),
                    in_=scr_v[h].rearrange("k p n -> p k n")), inc=sX, inc_n=16)

            load_kv(0, None)
            load_kv(1, None)
            exp_tok = {}
            msk_tok = {}
            pv_read_tok = None
            an_tok = None
            slot_rel = [None, None, None]
            slot_i = 0
            for h in range(H):
                pp = h % 2
                gslot = {}

                def emit_exp(h, gi, wit):
                    grp = EXP_GROUPS[gi]
                    sb = gslot[gi]
                    c0 = OFF[grp[0]]
                    c1 = c0 + sum(NKT[k2] for k2 in grp)
                    exp_tok[(h, gi)] = P.emit(
                        "scalar", [wit] + ([pv_read_tok] if gi == 0 else []),
                        lambda e, h=h, sb=sb, c0=c0, c1=c1: e.activation(
                            expT[:, c0:c1], ps_s[sb][:, 0:c1 - c0], AF.Exp,
                            bias=negm[:, h:h + 1], scale=SCALE),
                        inc=sA)
                    slot_rel[sb] = exp_tok[(h, gi)]
                    for kt in grp:
                        n = NKT[kt]
                        msk_tok[(h, kt)] = P.emit(
                            "vector", [exp_tok[(h, gi)]],
                            lambda e, kt=kt, n=n: e.tensor_mul(
                                expT[:, OFF[kt] + n - 128:OFF[kt] + n],
                                expT[:, OFF[kt] + n - 128:OFF[kt] + n],
                                maskb[:, kt * 128:(kt + 1) * 128]),
                            inc=sV)

                for gi, grp in enumerate(EXP_GROUPS):
                    sb = slot_i % 3
                    slot_i += 1
                    gslot[gi] = sb
                    w = [P.margin(kv_tok[h], sX), mask_tok, slot_rel[sb]]
                    tok = None
                    loc = 0
                    for kt in grp:
                        n = NKT[kt]
                        nmm = (n + 511) // 512
                        for c in range(nmm):
                            n0, n1 = c * 512, min(n, (c + 1) * 512)
                            tok = P.emit(
                                "tensor", w if tok is None else [],
                                lambda e, h=h, kt=kt, sb=sb, pp=pp, loc=loc,
                                n0=n0, n1=n1: e.matmul(
                                    ps_s[sb][:, loc + n0:loc + n1],
                                    kbuf[pp][:, kt * 128:(kt + 1) * 128],
                                    qT[:, h * 1024 + n0:h * 1024 + n1],
                                    start=True, stop=True),
                                inc=(sT if (kt == grp[-1] and c == nmm - 1)
                                     else None))
                        loc += n
                    if gi > 0:
                        emit_exp(h, gi - 1, tok)
                lsb = gslot[len(EXP_GROUPS) - 1]
                wlast = P.emit("tensor", [], lambda e, lsb=lsb: e.matmul(
                    ps_s[lsb][:, 768:832], ident[:], zeros64[:],
                    start=True, stop=True), inc=sT)
                emit_exp(h, len(EXP_GROUPS) - 1, wlast)
                tok = None
                for kt in range(NT):
                    n = NKT[kt]
                    w = [msk_tok[(h, kt)]]
                    if kt == 0:
                        w.append(an_tok)
                    nmm = (n + 511) // 512
                    for c in range(nmm):
                        n0, n1 = c * 512, min(n, (c + 1) * 512)
                        tok = P.emit("tensor", w if c == 0 else [],
                                     lambda e, kt=kt, pp=pp, n0=n0, n1=n1: e.matmul(
                                         ps_a[:, n0:n1],
                                         vbuf[pp][:, kt * 128:(kt + 1) * 128],
                                         expT[:, OFF[kt] + n0:OFF[kt] + n1],
                                         start=(kt == 0), stop=(kt == 15)),
                                     inc=(sT if (kt == 15 and c == nmm - 1) else None))
                pv_tok = tok
                dsbi = slot_i % 3
                slot_i += 1
                tok = None
                for kt in range(NT):
                    n = NKT[kt]
                    nmm = (n + 511) // 512
                    for c in range(nmm):
                        n0, n1 = c * 512, min(n, (c + 1) * 512)
                        tok = P.emit("tensor",
                                     [t_ones, slot_rel[dsbi]] if kt == 0 and c == 0
                                     else [],
                                     lambda e, kt=kt, dsbi=dsbi, n0=n0, n1=n1:
                                     e.matmul(
                                         ps_s[dsbi][0:1, n0:n1], ones_col[:],
                                         expT[:, OFF[kt] + n0:OFF[kt] + n1],
                                         start=(kt == 0), stop=(kt == 15)),
                                     inc=(sT if (kt == 15 and c == nmm - 1)
                                          else None))
                witd = P.emit("tensor", [], lambda e, dsbi=dsbi: e.matmul(
                    ps_s[dsbi][0:1, 0:64], ones_col[:], zeros64[:, 0:64],
                    start=False, stop=True, skip_group_check=True), inc=sT)
                pv_read_tok = witd
                dcp = P.emit("scalar", [witd], lambda e, dsbi=dsbi: e.copy(
                    dsb[:], ps_s[dsbi][0:1, :]), inc=sA)
                bc = None
                for c in range(2):
                    bc = P.emit("tensor", [dcp] if c == 0 else [],
                                lambda e, c=c, dsbi=dsbi: e.matmul(
                                    ps_s[dsbi][:, c * 512:(c + 1) * 512],
                                    ones_row_f[:], dsb[:, c * 512:(c + 1) * 512],
                                    start=True, stop=True),
                                inc=(sT if c == 1 else None))
                witb = P.emit("tensor", [], lambda e, dsbi=dsbi: e.matmul(
                    ps_s[dsbi][:, 0:64], ident[:], zeros64[:],
                    start=False, stop=True, skip_group_check=True), inc=sT)
                r1 = P.emit("scalar", [witb], lambda e, dsbi=dsbi: e.activation(
                    rb1[:], ps_s[dsbi][:], AF.Ln), inc=sA)
                slot_rel[dsbi] = r1
                r2 = P.emit("scalar", [r1], lambda e: e.activation(
                    rb2[:], rb1[:], AF.Exp, scale=-1.0), inc=sA)
                tok = None
                for c in range(2):
                    tok = P.emit("vector", [r2, pv_tok] if c == 0 else [],
                                 lambda e, h=h, c=c: e.tensor_mul(
                                     attnT[:, h * 1024 + c * 512:
                                           h * 1024 + (c + 1) * 512],
                                     ps_a[:, c * 512:(c + 1) * 512],
                                     rb2[:, c * 512:(c + 1) * 512]),
                                 inc=(sV if c == 1 else None))
                an_tok = tok
                if h + 2 < H:
                    load_kv(h + 2, an_tok)
        es_qT.close()
        barrier()

        with contextlib.ExitStack() as st3a:
            wob = [sbuf(st3a, f"wob{i}", [128, D], BF16) for i in range(WO_RING)]
            ystage = [sbuf(st3a, f"ystage{i}", [128, 2048], F32) for i in range(4)]
            ps_y = [st3a.enter_context(nc.psum_tensor(f"psy{i}", [128, 2048], F32))
                    for i in range(2)]
            wo_tok = {}
            ev_tok = {}
            ydma_tok = {}
            for h in range(WO_RING - 1):
                wo_tok[h] = P.emit("gpsimd", [], lambda e, h=h: e.dma_start(
                    out=wob[h % WO_RING][:], in_=wo[h * 128:(h + 1) * 128, :]),
                    inc=sG, inc_n=16)
            rnd = 0
            mm_rnd = {}

            def emit_y_evac(rnd2, wit):
                hb2, rr = divmod(rnd2, 16)
                j2, nh2 = divmod(rr, 2)
                sp = rnd2 % 4
                pp2 = rnd2 % 2
                ev_tok[rnd2] = P.emit(
                    "scalar", [wit, P.margin(ydma_tok.get(rnd2 - 4), sG)],
                    lambda e, sp=sp, pp2=pp2: e.copy(ystage[sp][:], ps_y[pp2][:]),
                    inc=sA)
                wprev = [ev_tok[rnd2]]
                if hb2 > 0:
                    wprev.append(ydma_tok[(hb2 - 1) * 16 + rr])
                kw = {"accum_op": ALU.add} if hb2 > 0 else {}
                ydma_tok[rnd2] = P.emit(
                    "gpsimd", wprev,
                    lambda e, j2=j2, nh2=nh2, sp=sp, kw=kw: e.dma_start(
                        out=scr_y[j2 * 128:(j2 + 1) * 128,
                                  nh2 * 2048:(nh2 + 1) * 2048],
                        in_=ystage[sp][:], **kw),
                    inc=sG, inc_n=16)

            for hb in range(4):
                for j in range(8):
                    for nh in range(2):
                        pp = rnd % 2
                        tok = None
                        for hi in range(8):
                            h = hb * 8 + hi
                            w = [P.margin(wo_tok[h], sG)]
                            if hi == 0:
                                w.append(ev_tok.get(rnd - 2))
                            for c in range(4):
                                tok = P.emit(
                                    "tensor", w if c == 0 else [],
                                    lambda e, h=h, j=j, nh=nh, c=c, pp=pp, hi=hi:
                                    e.matmul(
                                        ps_y[pp][:, c * 512:(c + 1) * 512],
                                        attnT[:, h * 1024 + j * 128:
                                              h * 1024 + (j + 1) * 128],
                                        wob[h % WO_RING][:, nh * 2048 + c * 512:
                                                         nh * 2048 + (c + 1) * 512],
                                        start=(hi == 0), stop=(hi == 7)),
                                    inc=(sT if (hi == 7 and c == 3) else None))
                        mm_rnd[rnd] = tok
                        if rnd > 0:
                            emit_y_evac(rnd - 1, tok)
                        rnd += 1
                if hb < 3:
                    for hi in range(8):
                        h = (hb + 1) * 8 + hi
                        if h < WO_RING - 1 or h >= H:
                            continue
                        wo_tok[h] = P.emit("gpsimd", [sT.tok],
                                           lambda e, h=h: e.dma_start(
                                               out=wob[h % WO_RING][:],
                                               in_=wo[h * 128:(h + 1) * 128, :]),
                                           inc=sG, inc_n=16)
            wity = P.emit("tensor", [], lambda e: e.matmul(
                ps_y[(rnd - 1) % 2][:, 0:64], ident[:], zeros64[:],
                start=False, stop=True, skip_group_check=True), inc=sT)
            emit_y_evac(63, wity)
            ydummy = P.emit("gpsimd", [ydma_tok[63]], lambda e: e.dma_start(
                out=scr_junk[0:1, 0:16], in_=ystage[3][0:1, 0:16]),
                inc=sG, inc_n=16)
            y_done = {}
            for j in range(8):
                y_done[j] = [P.margin(ydma_tok[3 * 16 + j * 2], sG),
                             P.margin(ydma_tok[3 * 16 + j * 2 + 1], sG)]
        es_attnT.close()
        barrier()

        es_x2 = contextlib.ExitStack()
        xln2T = sbuf(es_x2, "xln2T", [128, 32 * 1024], BF16, "left")
        with contextlib.ExitStack() as st3t:
            ybuf = [sbuf(st3t, f"ybuf{i}", [128, D], F32) for i in range(2)]
            xobuf = [sbuf(st3t, f"xobuf{i}", [128, D], F32) for i in range(2)]
            xatt_t = [sbuf(st3t, f"xatt{i}", [128, D], F32) for i in range(2)]
            xl2bf = [sbuf(st3t, f"xl2bf{i}", [128, D], BF16) for i in range(2)]
            sqscr = sbuf(st3t, "sqscr3t", [128, D], BF16)
            trp2 = [st3t.enter_context(nc.psum_tensor(f"trp2{i}", [128, 1024], BF16))
                    for i in range(2)]
            trpj = st3t.enter_context(nc.psum_tensor("trpj", [128, 128], BF16))
            buf_free = [None, None]
            cp2_done = {}
            xa_dma_tok = {}
            for j in range(8):
                pp = j % 2
                y_tok = P.emit("sync", y_done[j] + (buf_free[pp] or []),
                               lambda e, j=j, pp=pp: e.dma_start(
                                   out=ybuf[pp][:],
                                   in_=scr_y[j * 128:(j + 1) * 128, :]),
                               inc=sX, inc_n=16)
                xo_tok = P.emit("sync", [], lambda e, j=j, pp=pp: e.dma_start(
                    out=xobuf[pp][:], in_=x_own[j * 128:(j + 1) * 128, :]),
                    inc=sX, inc_n=16)
                tr, mn, rstd = ln_stats(ybuf[pp][:], [y_tok], pp * 6, sqscr)
                t1 = P.emit("vector", [tr], lambda e, pp=pp, mn=mn, rstd=rstd:
                            e.tensor_scalar(ybuf[pp][:], ybuf[pp][:], mn, rstd,
                                            ALU.add, ALU.mult), inc=sV)
                t2 = P.emit("vector", [t1, xo_tok], lambda e, pp=pp: e.tensor_add(
                    xatt_t[pp][:], ybuf[pp][:], xobuf[pp][:]), inc=sV)
                xa_dma = P.emit("sync", [t2], lambda e, j=j, pp=pp: e.dma_start(
                    out=scr_xatt[j * 128:(j + 1) * 128, :], in_=xatt_t[pp][:]),
                    inc=sX, inc_n=16)
                xa_dma_tok[j] = xa_dma
                tr2, mn2, rstd2 = ln_stats(xatt_t[pp][:], [t2], 12 + pp * 6, sqscr)
                t3 = P.emit("vector", [tr2], lambda e, pp=pp, mn2=mn2, rstd2=rstd2:
                            e.tensor_scalar(xl2bf[pp][:], xatt_t[pp][:], mn2, rstd2,
                                            ALU.add, ALU.mult), inc=sV)
                def emit_cp2(gidx, gp, j2, g, wit):
                    src = trp2[gp][:].rearrange("p (u n) -> p u n", n=128)
                    dst = xln2T[:].rearrange("p (a t) -> p a t", t=1024)[
                        :, g * 8:(g + 1) * 8, j2 * 128:(j2 + 1) * 128]
                    cp2_done[gidx] = P.emit(
                        "vector", [wit], lambda e, src=src, dst=dst:
                        e.tensor_copy(dst, src), inc=sV)

                tok = None
                for g in range(4):
                    gp = g % 2
                    w = [t3, cp2_done.get(j * 4 + g - 2)]
                    for u in range(8):
                        a = g * 8 + u
                        tok = P.emit("tensor", w if u == 0 else [],
                                     lambda e, pp=pp, a=a, gp=gp, u=u: e.transpose(
                                         trp2[gp][:, u * 128:(u + 1) * 128],
                                         xl2bf[pp][:, a * 128:(a + 1) * 128],
                                         ident[:]),
                                     inc=(sT if u == 7 else None))
                    if g > 0:
                        emit_cp2(j * 4 + g - 1, (g - 1) % 2, j, g - 1, tok)
                dummy2 = P.emit("tensor", [], lambda e, pp=pp: e.transpose(
                    trpj[:], xl2bf[pp][:, 0:128], ident[:]), inc=sT)
                emit_cp2(j * 4 + 3, 1, j, 3, dummy2)
                buf_free[pp] = [xa_dma, cp2_done[j * 4 + 3]]
        barrier()

        es_hts = contextlib.ExitStack()
        hts = sbuf(es_hts, "hts", [128, 32 * 1024], BF16, "left")
        with contextlib.ExitStack() as st3g:
            wgb = [sbuf(st3g, f"wgb{i}", [128, 32, 128], BF16) for i in range(3)]
            wub = [sbuf(st3g, f"wub{i}", [128, 32, 128], BF16) for i in range(3)]
            gs_bf = [sbuf(st3g, f"gsbf{i}", [128, 1024], BF16) for i in range(2)]
            ps_g = [st3g.enter_context(nc.psum_tensor(f"psg{i}", [128, 1024], F32))
                    for i in range(2)]
            ps_u = [st3g.enter_context(nc.psum_tensor(f"psu{i}", [128, 1024], F32))
                    for i in range(2)]
            wg_free = [None, None, None]
            toku_hist = {}
            hcp_tok = {}
            hdma_tok = {}
            sil_tok = {}
            strip_of = {}
            for (f0s, f1s) in FT_STRIPS:
                for ft in range(f0s, f1s):
                    strip_of[ft] = f0s

            def emit_hcp(ft, wit):
                f0 = strip_of[ft]
                fl = ft - f0
                pp = ft % 2
                hcp_tok[ft] = P.emit("vector",
                                     [wit, sil_tok[ft],
                                      hdma_tok.get(ft - 32) if f0 else None],
                                     lambda e, fl=fl, pp=pp: e.tensor_mul(
                                         hts[:, fl * 1024:(fl + 1) * 1024],
                                         ps_u[pp][:], gs_bf[pp][:]),
                                     inc=sV)
                wg_free[ft % 3] = toku_hist.get(ft)
                hdma_tok[ft] = P.emit("sync", [hcp_tok[ft]],
                                      lambda e, ft=ft, fl=fl: e.dma_start(
                                          out=scr_h[ft],
                                          in_=hts[:, fl * 1024:(fl + 1) * 1024]),
                                      inc=sX, inc_n=16)

            for ft in range(NFT):
                pp = ft % 2
                wr = ft % 3
                P.emit("gpsimd", [wg_free[wr]], lambda e, ft=ft, wr=wr:
                       e.dma_start(out=wgb[wr][:],
                                   in_=wgT[ft].rearrange("(a p) n -> p a n", p=128)),
                       inc=sG, inc_n=16)
                wg_tok = P.emit("gpsimd", [], lambda e, ft=ft, wr=wr:
                                e.dma_start(out=wub[wr][:],
                                            in_=wuT[ft].rearrange(
                                                "(a p) n -> p a n", p=128)),
                                inc=sG, inc_n=16)
                w = [P.margin(wg_tok, sG), hcp_tok.get(ft - 2)]
                tokg = None
                for dk in range(32):
                    for c in range(2):
                        tokg = P.emit(
                            "tensor", w if (dk == 0 and c == 0) else [],
                            lambda e, dk=dk, c=c, pp=pp, wr=wr: e.matmul(
                                ps_g[pp][:, c * 512:(c + 1) * 512],
                                wgb[wr][:, dk, :],
                                xln2T[:, dk * 1024 + c * 512:
                                      dk * 1024 + (c + 1) * 512],
                                start=(dk == 0), stop=(dk == 31)),
                            inc=(sT if (dk == 31 and c == 1) else None))
                if ft > 0:
                    emit_hcp(ft - 1, tokg)
                toku = None
                for dk in range(32):
                    for c in range(2):
                        toku = P.emit(
                            "tensor", [],
                            lambda e, dk=dk, c=c, pp=pp, wr=wr: e.matmul(
                                ps_u[pp][:, c * 512:(c + 1) * 512],
                                wub[wr][:, dk, :],
                                xln2T[:, dk * 1024 + c * 512:
                                      dk * 1024 + (c + 1) * 512],
                                start=(dk == 0), stop=(dk == 31)),
                            inc=(sT if (dk == 31 and c == 1) else None))
                toku_hist[ft] = toku
                sil_tok[ft] = P.emit("scalar", [toku, hcp_tok.get(ft - 2)],
                                     lambda e, pp=pp: e.activation(
                                         gs_bf[pp][:], ps_g[pp][:], AF.Silu),
                                     inc=sA)
            witu = P.emit("tensor", [], lambda e: e.matmul(
                ps_u[(NFT - 1) % 2][:, 0:64], ident[:], zeros64[:],
                start=False, stop=True, skip_group_check=True), inc=sT)
            emit_hcp(NFT - 1, witu)
            h_spill_done = sX.tok
        es_hts.close()
        es_x2.close()
        barrier()

        with contextlib.ExitStack() as st3d:
            hfull = sbuf(st3d, "hfull", [128, NFT * 1024], BF16)
            wdb = [sbuf(st3d, f"wdb{i}", [128, 512], BF16) for i in range(6)]
            y2st = [sbuf(st3d, f"y2st{i}", [128, 512], F32) for i in range(8)]
            ps_dn = [st3d.enter_context(
                nc.psum_tensor(f"psdn{i}", [128, 512], F32)) for i in range(8)]
            hh_tok = None
            for hc in range(8):
                f0h, f1h = hc * 11, min(NFT, (hc + 1) * 11)
                hh_tok = P.emit("scalar",
                                [h_spill_done] if hc == 0 else [],
                                lambda e, f0h=f0h, f1h=f1h: e.dma_start(
                                    out=hfull[:].rearrange(
                                        "p (f n) -> p f n", n=1024)[:, f0h:f1h, :],
                                    in_=scr_h[f0h:f1h].rearrange("f p n -> p f n")),
                                inc=sC, inc_n=16)
            ft_mm = {}
            ev_last = {}
            y2dma = {}
            for dc in range(8):
                for ft in range(NFT):
                    it = dc * NFT + ft
                    wp = it % 6
                    wd_tok = P.emit("gpsimd", [ft_mm.get(it - 6)],
                                    lambda e, dc=dc, ft=ft, wp=wp: e.dma_start(
                                        out=wdb[wp][:],
                                        in_=wdT[dc, ft * 128:(ft + 1) * 128, :]),
                                    inc=sG, inc_n=16)
                    w = [P.margin(wd_tok, sG), hh_tok]
                    if ft == 0:
                        w.append(ev_last.get(dc - 1))
                    tok = None
                    for qt in range(8):
                        tok = P.emit(
                            "tensor", w if qt == 0 else [],
                            lambda e, ft=ft, qt=qt, wp=wp: e.matmul(
                                ps_dn[qt][:],
                                hfull[:, ft * 1024 + qt * 128:
                                      ft * 1024 + (qt + 1) * 128],
                                wdb[wp][:],
                                start=(ft == 0), stop=(ft == NFT - 1)),
                            inc=(sT if qt == 7 else None))
                    ft_mm[it] = tok
                witdn = P.emit("tensor", [], lambda e: e.matmul(
                    ps_dn[7][:, 0:64], ident[:], zeros64[:],
                    start=False, stop=True, skip_group_check=True), inc=sT)
                last = None
                for qt in range(8):
                    evt = P.emit("scalar",
                                 [witdn, y2dma.get((dc - 1, qt))],
                                 lambda e, qt=qt: e.copy(
                                     y2st[qt][:], ps_dn[qt][:]),
                                 inc=sA)
                    last = P.emit("sync", [evt],
                                  lambda e, dc=dc, qt=qt: e.dma_start(
                                      out=scr_y2[qt * 128:(qt + 1) * 128,
                                                 dc * 512:(dc + 1) * 512],
                                      in_=y2st[qt][:]),
                                  inc=sX, inc_n=16)
                    y2dma[(dc, qt)] = last
                ev_last[dc] = evt
            P.emit("sync", [y2dma[(7, 7)]], lambda e: e.dma_start(
                out=scr_junk[0:1, 16:32], in_=y2st[7][0:1, 0:16]),
                inc=sX, inc_n=16)
            y2_done = sX.tok
        barrier()

        with contextlib.ExitStack() as st3f:
            y2buf = [sbuf(st3f, f"y2buf{i}", [128, D], F32) for i in range(2)]
            xabuf = [sbuf(st3f, f"xabuf{i}", [128, D], F32) for i in range(2)]
            obuf = [sbuf(st3f, f"obuf{i}", [128, D], F32) for i in range(2)]
            sqscr = sbuf(st3f, "sqscr3f", [128, D], BF16)
            odma = [None, None]
            for j in range(8):
                pp = j % 2
                y2_tok = P.emit("scalar", [y2_done, odma[pp]],
                                lambda e, j=j, pp=pp: e.dma_start(
                                    out=y2buf[pp][:],
                                    in_=scr_y2[j * 128:(j + 1) * 128, :]),
                                inc=sC, inc_n=16)
                xa_tok = P.emit("scalar", [P.margin(xa_dma_tok[j], sX)],
                                lambda e, j=j, pp=pp: e.dma_start(
                                    out=xabuf[pp][:],
                                    in_=scr_xatt[j * 128:(j + 1) * 128, :]),
                                inc=sC, inc_n=16)
                tr, mn, rstd = ln_stats(y2buf[pp][:], [y2_tok], pp * 6, sqscr)
                t1 = P.emit("vector", [tr], lambda e, pp=pp, mn=mn, rstd=rstd:
                            e.tensor_scalar(y2buf[pp][:], y2buf[pp][:], mn, rstd,
                                            ALU.add, ALU.mult), inc=sV)
                t2 = P.emit("vector", [t1, xa_tok], lambda e, pp=pp: e.tensor_add(
                    obuf[pp][:], y2buf[pp][:], xabuf[pp][:]), inc=sV)
                odma[pp] = P.emit("sync", [t2], lambda e, j=j, pp=pp: e.dma_start(
                    out=out_d[j * 128:(j + 1) * 128, :], in_=obuf[pp][:]),
                    inc=sX, inc_n=16)
            P.raw_wait("sync", sX.tok)

        with nc.Block() as block:
            block.tensor(lambda e: [f(e) for f in P.q["tensor"]])
            block.vector(lambda e: [f(e) for f in P.q["vector"]])
            block.scalar(lambda e: [f(e) for f in P.q["scalar"]])
            block.gpsimd(lambda e: [f(e) for f in P.q["gpsimd"]])
            block.sync(lambda e: [f(e) for f in P.q["sync"]])

    return nc


def kernel(L0_attn_hidden_states, L0_attn_wq, L0_attn_wkv, L0_attn_wo,
           L0_ffn_W_gate, L0_ffn_W_up, L0_ffn_W_down):
    x = np.asarray(L0_attn_hidden_states, dtype=np.float32)
    wq = np.asarray(L0_attn_wq, dtype=np.float32)
    wkv = np.asarray(L0_attn_wkv, dtype=np.float32)
    wo = np.asarray(L0_attn_wo, dtype=np.float32)
    wg = np.asarray(L0_ffn_W_gate, dtype=np.float32)
    wu = np.asarray(L0_ffn_W_up, dtype=np.float32)
    wd = np.asarray(L0_ffn_W_down, dtype=np.float32)

    wq_u = np.ascontiguousarray(wq.reshape(D, H, 192)[:, :, :128].transpose(1, 0, 2))
    wk_u = np.ascontiguousarray(wkv[:, :H * DN].reshape(D, H, 128).transpose(1, 0, 2))
    wv_u = np.ascontiguousarray(
        wkv[:, H * DN:H * (DN + DV)].reshape(D, H, 128).transpose(1, 0, 2))
    wgT = np.ascontiguousarray(wg.T.reshape(D, NFT, 128).transpose(1, 0, 2))
    wuT = np.ascontiguousarray(wu.T.reshape(D, NFT, 128).transpose(1, 0, 2))
    wdT = np.ascontiguousarray(wd.T.reshape(DFF, 8, 512).transpose(1, 0, 2))

    in_maps = []
    rows_by_core = []
    for c in range(8):
        b, par = c // 2, c % 2
        tiles_true, perm = core_layout(par)
        own_rows = np.concatenate(
            [np.arange(t * 128, (t + 1) * 128) for t in tiles_true])
        xb = x[b]
        in_maps.append({
            "x_own": np.ascontiguousarray(xb[own_rows]),
            "x_full": np.ascontiguousarray(
                np.concatenate([xb[t * 128:(t + 1) * 128] for t in perm], axis=0)),
            "wq": wq_u, "wk": wk_u, "wv": wv_u, "wo": wo,
            "wgT": wgT, "wuT": wuT, "wdT": wdT,
            "masks": build_masks(par),
        })
        rows_by_core.append((b, own_rows))

    nc = build_nc()
    res = run_bass_kernel_spmd(nc, in_maps, core_ids=list(range(8)))

    out = np.zeros((NB, S, D), dtype=np.float32)
    for c in range(8):
        b, own_rows = rows_by_core[c]
        out[b][own_rows] = res.results[c]["out"]
    return out
